# version 05
# speedup vs baseline: 1.6390x; 1.1501x over previous
import contextlib
import numpy as np
import ml_dtypes
import concourse.bass as bass
import concourse.mybir as mybir
from concourse.bass_utils import run_bass_kernel_spmd

F32 = mybir.dt.float32
BF16 = mybir.dt.bfloat16
AF = mybir.ActivationFunctionType
ALU = mybir.AluOpType
AXX = mybir.AxisListType.X
BF = ml_dtypes.bfloat16

D = 2048
S = 4096
NB = 4
NH = 16
HALF = S // 2
TT = 1024
NT = HALF // TT
NDC = D // 128
HID0 = 1792
HID1 = 1024
E0 = 8
E1 = 8
LN_EPS = 1e-5
LAM_INIT = 0.8 - 0.6 * float(np.exp(-0.3 * 12))
WBLK = 256



def _wt(w):
    return np.ascontiguousarray(np.asarray(w).T.astype(np.float32))


def _wt3(w):
    return np.ascontiguousarray(np.asarray(w).transpose(0, 2, 1).astype(np.float32))


def _bcol(b, n):
    return np.ascontiguousarray(np.asarray(b, np.float32).reshape(n, 128).T)


def host_consts():
    c = {}
    c["onec"] = np.ones((128, 1), BF)
    c["oner"] = np.ones((1, 128), BF)
    c["identb"] = np.eye(128, dtype=np.float32).astype(BF)
    c["identf"] = np.eye(128, dtype=np.float32)
    m = np.zeros((4, 128, 512), np.float32)
    for dd in range(4):
        for i in range(128):
            m[dd, i, 128 * dd + i:] = 1.0
    c["masks"] = np.ascontiguousarray(m.astype(BF))
    c["masks2"] = np.ascontiguousarray(
        np.concatenate([m, m], axis=2).astype(BF))
    return c



class Em:
    def __init__(self, nc, stack):
        self.nc = nc
        self.stack = stack
        self.sems = {}
        self.cnt = {}
        self.store_sems = set()
        self.slots = []
        self.wstreams = []
        for name in ("pe", "ac", "dv", "po"):
            self.sem(name)

    def snap_all(self):
        return (dict(self.cnt), [dict(s.pend) for s in self.slots],
                [w.i for w in self.wstreams])

    def restore_all(self, st):
        self.cnt = {k: st[0].get(k, 0)
                    for k in set(self.cnt) | set(st[0])}
        for s, p in zip(self.slots, st[1]):
            s.pend = dict(p)
        for w, i in zip(self.wstreams, st[2]):
            w.i = i

    def sem(self, name):
        if name not in self.sems:
            self.sems[name] = self.stack.enter_context(self.nc.semaphore(name))
            self.cnt[name] = 0
        return self.sems[name]

    def inc(self, name, instr, k=1):
        self.sem(name)
        instr.then_inc(self.sems[name], k)
        self.cnt[name] += k
        return instr

    def wait(self, eng, name, cnt=None):
        self.sem(name)
        eng.wait_ge(self.sems[name], self.cnt[name] if cnt is None else cnt)

    def final_store_waits(self, engs):
        for name in sorted(self.store_sems):
            for e in engs:
                e.wait_ge(self.sems[name], self.cnt[name])

    def dma(self, queue, out, in_, name, store=False):
        self.sem(name)
        if store:
            self.store_sems.add(name)
        self.inc(name, queue.dma_start(out=out, in_=in_), 16)
        return (name, self.cnt[name])

    def waitc(self, eng, tok):
        name, cnt = tok
        eng.wait_ge(self.sems[name], cnt)

    def dv(self, instr):
        self.inc("dv", instr)
        self.nc.vector.wait_ge(self.sems["dv"], self.cnt["dv"])
        return instr

    def acs(self, instr):
        self.inc("ac", instr)
        self.nc.scalar.wait_ge(self.sems["ac"], self.cnt["ac"])
        return instr


class Slot:
    def __init__(self, em):
        self.em = em
        self.pend = {}
        em.slots.append(self)

    def begin(self, eng):
        for name, cnt in self.pend.items():
            self.em.wait(eng, name, cnt)
        self.pend = {}

    def done(self, name):
        self.pend[name] = self.em.cnt[name]


class WStream:

    def __init__(self, em, name, maxfree, nbuf=2):
        self.em = em
        self.bufs = [em.stack.enter_context(
            em.nc.sbuf_tensor(f"{name}{i}", [128, maxfree], BF16))
            for i in range(nbuf)]
        self.slots = [Slot(em) for _ in range(nbuf)]
        self.i = 0
        em.wstreams.append(self)

    def load(self, src2d, nd, w):
        i = self.i % len(self.bufs)
        self.i += 1
        g = self.em.nc.gpsimd
        self.slots[i].begin(g)
        buf = self.bufs[i]
        sname = self.bufs[i].name
        step = 8
        for a0 in range(0, nd, step):
            n = min(step, nd - a0)
            src = src2d[a0 * 128:(a0 + n) * 128, :].rearrange(
                "(a p) h -> p a h", p=128)
            dst = buf[:, a0 * w:(a0 + n) * w].rearrange("p (a h) -> p a h", a=n)
            tok = self.em.dma(g, dst, src, sname)

        def view(dc, m0, mw):
            return buf[:, dc * w + m0:dc * w + m0 + mw]

        return view, tok, self.slots[i]



def emit_ln(em, C, x_dram, t0, T, xn):
    nc = em.nc
    t, v, a, g = nc.tensor, nc.vector, nc.scalar, nc.gpsimd
    em.sem("sacc")
    em.wait(v, "sacc")
    nh = T // 512
    pA, pB, pC = C["pA"], C["pB"], C["pC"]
    for dc in range(NDC):
        i = dc % 2
        xt, xts = C["xp"][i], C["xpslot"][i]
        xts.begin(g)
        tok = em.dma(g, xt[:, 0:T], x_dram[dc * 128:(dc + 1) * 128, t0:t0 + T],
                     f"xp{i}")
        em.waitc(t, tok)
        for h in range(nh):
            t.matmul(pC[0:1, h * 512:(h + 1) * 512], C["onec"][:, 0:1],
                     xt[:, h * 512:(h + 1) * 512],
                     start=(dc == 0), stop=(dc == NDC - 1))
        em.waitc(a, tok)
        C["sqslot"].begin(a)
        em.inc("ac", a.activation(C["sq"][:, 0:T], xt[:, 0:T], AF.Square))
        em.wait(t, "ac")
        for h in range(nh):
            i2 = t.matmul(pA[0:1, h * 512:(h + 1) * 512], C["onec"][:, 0:1],
                          C["sq"][:, h * 512:(h + 1) * 512],
                          start=(dc == 0), stop=(dc == NDC - 1))
        em.inc("pe", i2)
        C["sqslot"].done("pe")
        xts.done("pe")
        em.waitc(v, tok)
        em.inc("dv", v.tensor_copy(xn[:, dc * T:dc * T + T], xt[:, 0:T]))
        xts.done("dv")
    mu, r, abf, bbf = C["mu"], C["r"], C["abf"], C["bbf"]
    em.wait(v, "pe")
    em.dv(v.tensor_scalar_mul(mu[0:1, 0:T], pC[0:1, 0:T], 1.0 / D))
    em.dv(v.tensor_mul(r[0:1, 0:T], mu[0:1, 0:T], mu[0:1, 0:T]))
    em.dv(v.scalar_tensor_tensor(r[0:1, 0:T], pA[0:1, 0:T], 1.0 / D,
                                 r[0:1, 0:T], ALU.mult, ALU.subtract))
    em.wait(a, "dv")
    em.acs(a.activation(r[0:1, 0:T], r[0:1, 0:T], AF.Ln,
                        bias=C["epsc"][0:1, 0:1]))
    em.acs(a.activation(r[0:1, 0:T], r[0:1, 0:T], AF.Exp, scale=-0.5))
    em.wait(v, "ac")
    em.dv(v.tensor_copy(abf[0:1, 0:T], r[0:1, 0:T]))
    em.dv(v.scalar_tensor_tensor(bbf[0:1, 0:T], mu[0:1, 0:T], -1.0,
                                 r[0:1, 0:T], ALU.mult, ALU.mult))
    em.wait(t, "dv")
    for h in range(nh):
        t.matmul(pA[:, h * 512:(h + 1) * 512], C["oner"][0:1, :],
                 abf[0:1, h * 512:(h + 1) * 512], start=True, stop=True)
        i2 = t.matmul(pB[:, h * 512:(h + 1) * 512], C["oner"][0:1, :],
                      bbf[0:1, h * 512:(h + 1) * 512], start=True, stop=True)
    em.inc("pe", i2)
    em.wait(v, "pe")
    em.dv(v.tensor_copy(C["absb"][:, 0:T], pA[:, 0:T]))
    em.dv(v.tensor_copy(C["bbsb"][:, 0:T], pB[:, 0:T]))
    for dc in range(NDC):
        em.dv(v.tensor_mul(xn[:, dc * T:dc * T + T],
                           xn[:, dc * T:dc * T + T], C["absb"][:, 0:T]))
        em.dv(v.tensor_add(xn[:, dc * T:dc * T + T],
                           xn[:, dc * T:dc * T + T], C["bbsb"][:, 0:T]))


def emit_gate0(em, C, xn, T, cwT):
    nc = em.nc
    t, v, a = nc.tensor, nc.vector, nc.scalar
    ntt = T // 128
    pE, pA = C["pE"], C["pA"]
    em.wait(t, "dv")
    for k in range(ntt):
        for dc in range(NDC):
            ins = t.matmul(pE[:, k * 8:(k + 1) * 8],
                           xn[:, dc * T + k * 128:dc * T + (k + 1) * 128],
                           C["embT"][:, dc * 8:(dc + 1) * 8],
                           start=(dc == 0), stop=(dc == NDC - 1))
    em.inc("pe", ins)
    em.wait(a, "pe")
    em.inc("ac", a.activation(C["sg"][:, 0:ntt * 8], pE[:, 0:ntt * 8], AF.Sigmoid))
    em.wait(v, "ac")
    cw, g1, g2, m1, m2 = C["cw"], C["g1"], C["g2"], C["m1"], C["m2"]
    g3 = C["g3"]
    for k in range(ntt):
        sgk = C["sg"][:, k * 8:(k + 1) * 8]
        cwk = cw[:, k * 8:(k + 1) * 8]
        em.dv(v.tensor_add(g1[:, 0:8], sgk, C["gb"][:, 0:8]))
        em.dv(v.reduce_max(m1[:, 0:1], g1[:, 0:8], axis=AXX))
        em.dv(v.tensor_single_scalar(g2[:, 0:8], g1[:, 0:8], m1[:, 0:1],
                                     ALU.is_ge))
        em.dv(v.scalar_tensor_tensor(g3[:, 0:8], g2[:, 0:8], -1e30,
                                     g1[:, 0:8], ALU.mult, ALU.add))
        em.dv(v.reduce_max(m2[:, 0:1], g3[:, 0:8], axis=AXX))
        em.dv(v.tensor_single_scalar(g2[:, 0:8], g1[:, 0:8], m2[:, 0:1],
                                     ALU.is_ge))
        em.dv(v.tensor_mul(cwk, sgk, g2[:, 0:8]))
        em.dv(v.reduce_sum(m1[:, 0:1], cwk, axis=AXX))
        em.dv(v.reciprocal(m2[:, 0:1], m1[:, 0:1]))
        em.dv(v.tensor_single_scalar(cwk, cwk, m2[:, 0:1], ALU.mult))
    _cw_transpose(em, C, cw, ntt, cwT)


def _cw_transpose(em, C, cw, ntt, cwT):
    nc = em.nc
    t, v = nc.tensor, nc.vector
    pB = C["pB"]
    em.wait(t, "dv")
    for k in range(ntt):
        kk = k % 8
        if kk == 0:
            C["pBslot"].begin(t)
        ins = t.transpose(pB[0:8, kk * 128:(kk + 1) * 128],
                          cw[:, k * 8:(k + 1) * 8], C["identf"][:, :])
        em.inc("pe", ins)
        if kk == 7 or k == ntt - 1:
            em.wait(v, "pe")
            base = (k // 8) * 1024
            n = (kk + 1) * 128
            em.inc("dv", v.tensor_copy(cwT[0:8, base:base + n], pB[0:8, 0:n]))
            C["pBslot"].done("dv")


def emit_proj(em, C, ws, wsrc2d, xn, T, dout, out_handler):
    nc = em.nc
    t = nc.tensor
    nh = T // 512
    nblk = dout // WBLK
    mper = WBLK // 128
    for b in range(nblk):
        view, tok, slot = ws.load(wsrc2d[:, b * WBLK:(b + 1) * WBLK], NDC, WBLK)
        for m in range(mper):
            ot = b * mper + m
            ps = C["pA"] if ot % 2 == 0 else C["pB"]
            oslot = C["pAslot"] if ot % 2 == 0 else C["pBslot"]
            oslot.begin(t)
            em.waitc(t, tok)
            for h in range(nh):
                for dc in range(NDC):
                    ins = t.matmul(
                        ps[:, h * 512:(h + 1) * 512], view(dc, m * 128, 128),
                        xn[:, dc * T + h * 512:dc * T + (h + 1) * 512],
                        start=(dc == 0), stop=(dc == NDC - 1))
            em.inc("pe", ins)
            slot.done("pe")
            out_handler(ot, ps, oslot)


def emit_ffn0(em, C, xn, x_dram, t0, T, cwT, wgT, wdT, wuT, swgT, swdT, swuT,
              acc, acc_dram):
    nc = em.nc
    t, v, a, g = nc.tensor, nc.vector, nc.scalar, nc.gpsimd
    em.sem("sacc")
    em.wait(v, "sacc")
    for s2 in ("sq20", "sq21"):
        em.sem(s2)
        em.wait(a, s2)
    nh = T // 512
    nhid = HID0 // 128
    wsA, wsB = C["wsA"], C["wsB"]
    rot = [("pA", C["pA"]), ("pB", C["pB"]), ("pC", C["pC"])]
    rslot = [C["pAslot"], C["pBslot"], C["pCslot"]]

    for e in range(E0 + 1):
        shared = (e == E0)
        wgs = swgT if shared else wgT[e]
        wds = swdT if shared else wdT[e]
        wus = swuT if shared else wuT[e]
        if not shared:
            sy = nc.sync
            em.wait(sy, "dv")
            C["cwrowslot"].begin(sy)
            rc = em.dma(sy, C["cwrow"][0:1, 0:T], cwT[e:e + 1, 0:T], "cwr")
            em.wait(t, "dv")
            em.waitc(t, rc)
            C["pCslot"].begin(t)
            for h in range(nh):
                ins = t.matmul(C["pC"][:, h * 512:(h + 1) * 512],
                               C["oner"][0:1, :],
                               C["cwrow"][0:1, h * 512:(h + 1) * 512],
                               start=True, stop=True)
            C["cwrowslot"].done("pe")
            em.inc("pe", ins)
            em.wait(v, "pe")
            em.dv(v.tensor_copy(C["cwb"][:, 0:T], C["pC"][:, 0:T]))
            C["pCslot"].done("dv")
        for hh in range(nhid):
            bk, m = divmod(hh, WBLK // 128)
            if m == 0:
                vg, cg, sg_ = wsA.load(wgs[:, bk * WBLK:(bk + 1) * WBLK], NDC,
                                       WBLK)
                vd, cd, sd_ = wsB.load(wds[:, bk * WBLK:(bk + 1) * WBLK], NDC,
                                       WBLK)
            ia_i = (2 * hh) % 3
            ib_i = (2 * hh + 1) % 3
            pa, sa = rot[ia_i][1], rslot[ia_i]
            pb, sb_ = rot[ib_i][1], rslot[ib_i]
            sa.begin(t)
            em.waitc(t, cg)
            for h in range(nh):
                for dc in range(NDC):
                    t.matmul(pa[:, h * 512:(h + 1) * 512], vg(dc, m * 128, 128),
                             xn[:, dc * T + h * 512:dc * T + (h + 1) * 512],
                             start=(dc == 0), stop=(dc == NDC - 1))
            sb_.begin(t)
            em.waitc(t, cd)
            for h in range(nh):
                for dc in range(NDC):
                    ib = t.matmul(pb[:, h * 512:(h + 1) * 512],
                                  vd(dc, m * 128, 128),
                                  xn[:, dc * T + h * 512:dc * T + (h + 1) * 512],
                                  start=(dc == 0), stop=(dc == NDC - 1))
            em.inc("pe", ib)
            sg_.done("pe")
            sd_.done("pe")
            em.wait(a, "pe")
            C["saslot"].begin(a)
            em.inc("ac", a.activation(C["sa"][:, 0:T], pa[:, 0:T], AF.Silu))
            sa.done("ac")
            em.wait(v, "ac")
            hp = C["hp"][:, hh * T:hh * T + T]
            if shared:
                em.dv(v.tensor_mul(hp, C["sa"][:, 0:T], pb[:, 0:T]))
            else:
                em.dv(v.tensor_mul(C["tmp"][:, 0:T], C["sa"][:, 0:T],
                                   pb[:, 0:T]))
                em.dv(v.tensor_mul(hp, C["tmp"][:, 0:T], C["cwb"][:, 0:T]))
            C["saslot"].done("dv")
            sb_.done("dv")
        for bk in range(D // WBLK):
            vu, cu, su_ = wsA.load(wus[:, bk * WBLK:(bk + 1) * WBLK], nhid, WBLK)
            for m in range(WBLK // 128):
                ot = bk * (WBLK // 128) + m
                ri = ot % 2
                po, so = rot[ri][1], rslot[ri]
                so.begin(t)
                em.waitc(t, cu)
                em.wait(t, "dv")
                for h in range(nh):
                    for hc in range(nhid):
                        io = t.matmul(
                            po[:, h * 512:(h + 1) * 512], vu(hc, m * 128, 128),
                            C["hp"][:, hc * T + h * 512:hc * T + (h + 1) * 512],
                            start=(hc == 0), stop=(hc == nhid - 1))
                em.inc("pe", io)
                su_.done("pe")
                em.wait(v, "pe")
                accm = acc[:, ot * T:ot * T + T]
                if e == 0:
                    xb, xbs = C["xp"][0], C["xpslot"][0]
                    xbs.begin(g)
                    tok = em.dma(g, xb[:, 0:T],
                                 x_dram[ot * 128:(ot + 1) * 128, t0:t0 + T],
                                 "xp0")
                    em.waitc(v, tok)
                    em.dv(v.tensor_add(accm, po[:, 0:T], xb[:, 0:T]))
                    xbs.done("dv")
                else:
                    em.dv(v.tensor_add(accm, po[:, 0:T], accm))
                so.done("dv")
    em.wait(g, "dv")
    for q in range(4):
        src = acc[:, q * 4 * T:(q + 1) * 4 * T].rearrange("p (a t) -> p a t", a=4)
        dst = acc_dram[q * 512:(q + 1) * 512, t0:t0 + T].rearrange(
            "(a p) t -> p a t", p=128)
        em.dma(g, dst, src, "sacc", store=True)



def build_A():
    nc = bass.Bass("TRN2", target_bir_lowering=False)
    ap = lambda n, shp, dt, kind: nc.dram_tensor(n, shp, dt, kind=kind).ap()
    x0 = ap("x0", [D, HALF], F32, "ExternalInput")
    wqkvT = ap("wqkvT", [D, 3 * D], F32, "ExternalInput")
    embT = ap("embT", [D, 8], F32, "ExternalInput")
    gbi = ap("gb", [128, 8], F32, "ExternalInput")
    wgT = ap("wgT", [E0, D, HID0], F32, "ExternalInput")
    wdT = ap("wdT", [E0, D, HID0], F32, "ExternalInput")
    wuT = ap("wuT", [E0, HID0, D], F32, "ExternalInput")
    swgT = ap("swgT", [D, HID0], F32, "ExternalInput")
    swdT = ap("swdT", [D, HID0], F32, "ExternalInput")
    swuT = ap("swuT", [HID0, D], F32, "ExternalInput")
    onec = ap("onec", [128, 1], BF16, "ExternalInput")
    oner = ap("oner", [1, 128], BF16, "ExternalInput")
    identf = ap("identf", [128, 128], F32, "ExternalInput")
    identb = ap("identb", [128, 128], BF16, "ExternalInput")
    epsc = ap("epsc", [128, 1], F32, "ExternalInput")

    q0 = ap("q0", [D, HALF], BF16, "ExternalOutput")
    k0 = ap("k0", [D, HALF], BF16, "ExternalOutput")
    v0 = ap("v0", [HALF, D], BF16, "ExternalOutput")
    acc0 = ap("acc0", [D, HALF], BF16, "ExternalOutput")

    with contextlib.ExitStack() as stack:
        em = Em(nc, stack)
        sb = lambda n, shp, dt=BF16: stack.enter_context(nc.sbuf_tensor(n, shp, dt))
        psa = lambda n, shp, dt=F32: stack.enter_context(
            nc.psum_tensor(n, shp, dt))

        C = {}
        C["pA"] = psa("pA", [128, 1024])
        C["pB"] = psa("pB", [128, 1024])
        C["pC"] = psa("pC", [128, 1024])
        C["pVT"] = psa("pVT", [128, 1024], BF16)
        C["pE"] = psa("pE", [128, 512])
        for nme in ("pA", "pB", "pC", "pVT", "pE"):
            C[nme + "slot"] = Slot(em)

        xn = sb("xn", [128, NDC * TT])
        acc = sb("acc", [128, NDC * TT], F32)
        C["hp"] = sb("hp", [128, (HID0 // 128) * TT])
        C["xp"] = [sb("xp0", [128, TT]), sb("xp1", [128, TT])]
        C["xpslot"] = [Slot(em), Slot(em)]
        C["sq"] = sb("sq", [128, TT])
        C["sqslot"] = Slot(em)
        C["sa"] = sb("sa", [128, TT])
        C["saslot"] = Slot(em)
        C["tmp"] = sb("tmp", [128, TT])
        C["cwb"] = sb("cwb", [128, TT])
        C["absb"] = sb("absb", [128, TT])
        C["bbsb"] = sb("bbsb", [128, TT])
        cwT = sb("cwT", [8, TT])
        C["mu"] = sb("mu", [1, TT], F32)
        C["r"] = sb("r", [1, TT], F32)
        C["abf"] = sb("abf", [1, TT])
        C["bbf"] = sb("bbf", [1, TT])
        C["sg"] = sb("sgb", [128, 64], F32)
        C["cw"] = sb("cwb2", [128, 64], F32)
        C["g1"] = sb("g1", [128, 8], F32)
        C["g2"] = sb("g2", [128, 8], F32)
        C["g3"] = sb("g3", [128, 8], F32)
        C["m1"] = sb("m1", [128, 1], F32)
        C["m2"] = sb("m2", [128, 1], F32)
        C["onec"] = sb("onecs", [128, 1])
        C["oner"] = sb("oners", [1, 128])
        C["identf"] = sb("idf", [128, 128], F32)
        C["identb"] = sb("idb", [128, 128])
        C["embT"] = sb("embs", [128, NDC * 8])
        C["gb"] = sb("gbs", [128, 8], F32)
        C["epsc"] = sb("epscs", [128, 1], F32)
        C["cwrow"] = sb("cwrow", [1, TT])
        C["cwrowslot"] = Slot(em)
        evq = sb("evq", [128, TT])
        evqs = Slot(em)
        evv = sb("evv", [128, TT])
        evvs = Slot(em)
        vtm = sb("vtm", [128, TT])
        vtms = Slot(em)

        C["wsA"] = WStream(em, "wsA", NDC * WBLK)
        C["wsB"] = WStream(em, "wsB", NDC * WBLK)
        wsQ = C["wsA"]

        with nc.Block() as block:
            @block.tensor
            def _(t):
                g, v, a, sy = nc.gpsimd, nc.vector, nc.scalar, nc.sync
                em.dma(g, C["onec"][:, :], onec, "cst")
                em.dma(g, C["oner"][0:1, :], oner, "cst")
                em.dma(g, C["identf"][:, :], identf, "cst")
                em.dma(g, C["identb"][:, :], identb, "cst")
                em.dma(g, C["embT"][:, :].rearrange("p (a h) -> p a h", a=NDC),
                       embT.rearrange("(a p) h -> p a h", p=128), "cst")
                em.dma(g, C["gb"][:, :], gbi, "cst")
                em.dma(g, C["epsc"][:, :], epsc, "cst")
                for e_ in (nc.tensor, v, a):
                    em.wait(e_, "cst")

                for tt in range(NT):
                    t0 = tt * TT
                    nc.all_engine_barrier()
                    emit_ln(em, C, x0, t0, TT, xn)
                    nc.all_engine_barrier()
                    emit_gate0(em, C, xn, TT, cwT)
                    nc.all_engine_barrier()

                    def qkv_handler(ot, pso, oslot):
                        s, head = divmod(ot, NH)
                        if s < 2:
                            dst = q0 if s == 0 else k0
                            em.wait(v, "pe")
                            evqs.begin(v)
                            em.inc("dv", v.tensor_copy(evq[:, 0:TT],
                                                       pso[:, 0:TT]))
                            oslot.done("dv")
                            em.wait(sy, "dv")
                            em.dma(sy, dst[head * 128:(head + 1) * 128,
                                           t0:t0 + TT], evq[:, 0:TT], "sq1",
                                   store=True)
                            evqs.done("sq1")
                        else:
                            em.wait(v, "pe")
                            evvs.begin(v)
                            em.inc("dv", v.tensor_copy(evv[:, 0:TT],
                                                       pso[:, 0:TT]))
                            oslot.done("dv")
                            em.wait(t, "dv")
                            C["pVTslot"].begin(t)
                            for i in range(TT // 128):
                                ins = t.transpose(
                                    C["pVT"][:, i * 128:(i + 1) * 128],
                                    evv[:, i * 128:(i + 1) * 128],
                                    C["identb"][:, :])
                            em.inc("pe", ins)
                            evvs.done("pe")
                            em.wait(v, "pe")
                            vtms.begin(v)
                            em.inc("dv", v.tensor_copy(vtm[:, 0:TT],
                                                       C["pVT"][:, 0:TT]))
                            C["pVTslot"].done("dv")
                            em.wait(sy, "dv")
                            em.dma(sy,
                                   v0[t0:t0 + TT, head * 128:(head + 1) * 128]
                                   .rearrange("(i p) d -> p i d", p=128),
                                   vtm[:, 0:TT].rearrange("p (i d) -> p i d",
                                                          d=128), "sq2",
                                   store=True)
                            vtms.done("sq2")

                    emit_proj(em, C, wsQ, wqkvT, xn, TT, 3 * D, qkv_handler)
                    nc.all_engine_barrier()
                    emit_ffn0(em, C, xn, x0, t0, TT, cwT,
                              [wgT[e] for e in range(E0)],
                              [wdT[e] for e in range(E0)],
                              [wuT[e] for e in range(E0)],
                              swgT, swdT, swuT, acc, acc0)
                nc.all_engine_barrier()
                em.final_store_waits([nc.gpsimd, nc.sync])
    return nc


def prep_A_inputs(inputs, core):
    b, h = divmod(core, 2)
    x = np.asarray(inputs["L0_attn_x"])
    xfm = np.ascontiguousarray(x[b, h * HALF:(h + 1) * HALF, :].T)
    c = host_consts()
    return {
        "x0": xfm.astype(np.float32),
        "wqkvT": _wt(inputs["L0_attn_W_qkv"]),
        "embT": _wt(inputs["L0_ffn_expert_emb"]),
        "gb": np.ascontiguousarray(
            np.broadcast_to(np.asarray(inputs["L0_ffn_gate_bias"],
                                       np.float32)[None, :], (128, 8))),
        "wgT": _wt3(inputs["L0_ffn_Wg"]),
        "wdT": _wt3(inputs["L0_ffn_Wd"]),
        "wuT": _wt3(inputs["L0_ffn_Wu"]),
        "swgT": _wt(inputs["L0_ffn_sWg"]),
        "swdT": _wt(inputs["L0_ffn_sWd"]),
        "swuT": _wt(inputs["L0_ffn_sWu"]),
        "onec": c["onec"], "oner": c["oner"],
        "identf": c["identf"], "identb": c["identb"],
        "epsc": np.full((128, 1), LN_EPS, np.float32),
    }






def build_B():
    nc = bass.Bass("TRN2", target_bir_lowering=False)
    ap = lambda n, shp, dt, kind: nc.dram_tensor(n, shp, dt, kind=kind).ap()
    qd = ap("qc", [8 * 128, S], BF16, "ExternalInput")
    kd = ap("kc", [8 * 128, S], BF16, "ExternalInput")
    vd = ap("vc", [S, 8 * 128], BF16, "ExternalInput")
    lq1 = ap("lq1", [128, 1], F32, "ExternalInput")
    lk1 = ap("lk1", [128, 1], F32, "ExternalInput")
    lq2 = ap("lq2", [128, 1], F32, "ExternalInput")
    lk2 = ap("lk2", [128, 1], F32, "ExternalInput")
    ggm = ap("ggm", [128, 8], F32, "ExternalInput")
    gbt = ap("gbt", [128, 8], F32, "ExternalInput")
    masks = ap("masks", [4, 128, 512], BF16, "ExternalInput")
    onec = ap("onec", [128, 1], BF16, "ExternalInput")
    oner = ap("oner", [1, 128], BF16, "ExternalInput")
    epsc = ap("epsc", [128, 1], F32, "ExternalInput")
    attn = ap("attn", [8 * 128, S], BF16, "ExternalOutput")

    NQG = S // 512
    with contextlib.ExitStack() as stack:
        em = Em(nc, stack)
        sb = lambda n, shp, dt=BF16: stack.enter_context(nc.sbuf_tensor(n, shp, dt))
        psa = lambda n, shp, dt=F32: stack.enter_context(nc.psum_tensor(n, shp, dt))

        pS = [psa(f"pS{i}", [128, 512]) for i in range(4)]
        pSs = [Slot(em) for _ in range(4)]
        pA1 = psa("pA1", [128, 512])
        pA2 = psa("pA2", [128, 512])
        pl1 = psa("pl1", [128, 512])
        pl2 = psa("pl2", [128, 512])
        pA1s, pA2s, pl1s, pl2s = (Slot(em) for _ in range(4))

        qsb = [sb(f"qsb{i}", [128, S]) for i in range(2)]
        ksb = [sb(f"ksb{i}", [128, S]) for i in range(2)]
        vsb = [sb(f"vsb{i}", [128, S]) for i in range(2)]
        ldslot = [Slot(em), Slot(em)]
        P1 = [sb(f"P1{i}", [128, 512]) for i in range(2)]
        P2 = [sb(f"P2{i}", [128, 512]) for i in range(2)]
        P1s = [Slot(em), Slot(em)]
        P2s = [Slot(em), Slot(em)]
        msk_sb = sb("msk", [128, 4 * 512])
        onec_sb = sb("onecs", [128, 1])
        oner_sb = sb("oners", [1, 128])
        epsc_sb = sb("epscs", [128, 1], F32)
        lam_sb = [sb(f"lam{i}", [128, 1], F32) for i in range(4)]
        neglam = sb("neglam", [128, 1], F32)
        lsc1 = sb("lsc1", [128, 1], F32)
        lsc2 = sb("lsc2", [128, 1], F32)
        ggm_sb = sb("ggms", [128, 8], F32)
        gbt_sb = sb("gbts", [128, 8], F32)
        r1row = sb("r1row", [1, 512], F32)
        r2row = sb("r2row", [1, 512], F32)
        r1bf = sb("r1bf", [1, 512])
        r2bf = sb("r2bf", [1, 512])
        r1bs = sb("r1bs", [128, 512], F32)
        r2bs = sb("r2bs", [128, 512], F32)
        t1f = sb("t1f", [128, 512], F32)
        t2f = sb("t2f", [128, 512], F32)
        tAf = sb("tAf", [128, 512], F32)
        Abf = sb("Abf", [128, 512])
        sqA = sb("sqA", [128, 512])
        gmu = sb("gmu", [1, 512], F32)
        gsc = sb("gsc", [1, 512], F32)
        grr = sb("grr", [1, 512], F32)
        gabf = sb("gabf", [1, 512])
        gbbf = sb("gbbf", [1, 512])
        gab = sb("gab", [128, 512], F32)
        gbb = sb("gbb", [128, 512], F32)
        afin = sb("afin", [128, 512])
        afins = Slot(em)

        with nc.Block() as block:
            @block.tensor
            def _(t):
                g, v, a, sy = nc.gpsimd, nc.vector, nc.scalar, nc.sync
                em.dma(g, onec_sb[:, :], onec, "cst")
                em.dma(g, oner_sb[0:1, :], oner, "cst")
                em.dma(g, epsc_sb[:, :], epsc, "cst")
                em.dma(g, msk_sb[:, :].rearrange("p (a x) -> p a x", a=4),
                       masks.rearrange("a p x -> p a x"), "cst")
                em.dma(g, lam_sb[0][:, :], lq1, "cst")
                em.dma(g, lam_sb[1][:, :], lk1, "cst")
                em.dma(g, lam_sb[2][:, :], lq2, "cst")
                em.dma(g, lam_sb[3][:, :], lk2, "cst")
                em.dma(g, ggm_sb[:, :], ggm, "cst")
                em.dma(g, gbt_sb[:, :], gbt, "cst")
                em.wait(v, "cst")
                em.wait(a, "cst")
                em.wait(t, "cst")
                em.dv(v.tensor_mul(lsc1[:, 0:1], lam_sb[0][:, 0:1],
                                   lam_sb[1][:, 0:1]))
                em.dv(v.tensor_mul(lsc2[:, 0:1], lam_sb[2][:, 0:1],
                                   lam_sb[3][:, 0:1]))
                em.wait(a, "dv")
                em.acs(a.activation(lsc1[:, 0:1], lsc1[:, 0:1], AF.Exp))
                em.acs(a.activation(lsc2[:, 0:1], lsc2[:, 0:1], AF.Exp))
                em.wait(v, "ac")
                em.dv(v.tensor_sub(neglam[:, 0:1], lsc1[:, 0:1], lsc2[:, 0:1]))
                em.dv(v.tensor_scalar_add(neglam[:, 0:1], neglam[:, 0:1],
                                          LAM_INIT))
                em.dv(v.tensor_scalar_mul(neglam[:, 0:1], neglam[:, 0:1], -1.0))
                em.dv(v.tensor_scalar_mul(ggm_sb[:, 0:8], ggm_sb[:, 0:8],
                                          1.0 - LAM_INIT))
                em.dv(v.tensor_scalar_mul(gbt_sb[:, 0:8], gbt_sb[:, 0:8],
                                          1.0 - LAM_INIT))

                def load_head(hi):
                    i = hi % 2
                    ldslot[i].begin(g)
                    em.dma(g, qsb[i][:, :], qd[hi * 128:(hi + 1) * 128, :],
                           f"hl{i}")
                    em.dma(g, ksb[i][:, :], kd[hi * 128:(hi + 1) * 128, :],
                           f"hl{i}")
                    for part in range(4):
                        tok = em.dma(
                            g,
                            vsb[i][:, part * 1024:(part + 1) * 1024].rearrange(
                                "p (kt d) -> p kt d", d=128),
                            vd[part * 1024:(part + 1) * 1024,
                               hi * 128:(hi + 1) * 128].rearrange(
                                "(kt p) d -> p kt d", p=128), f"hl{i}")
                    return tok

                hcnt = [load_head(0)]
                for hi in range(8):
                    if hi < 7:
                        hcnt.append(load_head(hi + 1))
                    i = hi % 2
                    q_, k_, v_ = qsb[i], ksb[i], vsb[i]
                    for qg in range(NQG):
                        kts = 4 * qg + 4
                        pA1s.begin(t)
                        pA2s.begin(t)
                        pl1s.begin(t)
                        pl2s.begin(t)
                        for kt in range(kts):
                            si = kt % 2
                            S1, S2 = pS[2 * si], pS[2 * si + 1]
                            pSs[2 * si].begin(t)
                            pSs[2 * si + 1].begin(t)
                            em.waitc(t, hcnt[hi])
                            t.matmul(S1[:, :], k_[0:64, kt * 128:(kt + 1) * 128],
                                     q_[0:64, qg * 512:(qg + 1) * 512],
                                     start=True, stop=True)
                            ins = t.matmul(S2[:, :],
                                           k_[64:128, kt * 128:(kt + 1) * 128],
                                           q_[64:128, qg * 512:(qg + 1) * 512],
                                           start=True, stop=True)
                            em.inc("pe", ins)
                            em.wait(a, "pe")
                            P1s[si].begin(a)
                            P2s[si].begin(a)
                            em.inc("ac", a.activation(P1[si][:, :], S1[:, :],
                                                      AF.Exp, scale=0.125))
                            em.inc("ac", a.activation(P2[si][:, :], S2[:, :],
                                                      AF.Exp, scale=0.125))
                            pSs[2 * si].done("ac")
                            pSs[2 * si + 1].done("ac")
                            if kt >= kts - 4:
                                didx = kt - 4 * qg
                                mk = msk_sb[:, didx * 512:(didx + 1) * 512]
                                em.wait(v, "ac")
                                em.dv(v.tensor_mul(P1[si][:, :], P1[si][:, :], mk))
                                em.dv(v.tensor_mul(P2[si][:, :], P2[si][:, :], mk))
                                em.wait(t, "dv")
                            else:
                                em.wait(t, "ac")
                            st_ = (kt == 0)
                            sp_ = (kt == kts - 1)
                            t.matmul(pl1[0:1, :], onec_sb[:, 0:1], P1[si][:, :],
                                     start=st_, stop=sp_)
                            t.matmul(pl2[0:1, :], onec_sb[:, 0:1], P2[si][:, :],
                                     start=st_, stop=sp_)
                            t.matmul(pA1[:, :], v_[:, kt * 128:(kt + 1) * 128],
                                     P1[si][:, :], start=st_, stop=sp_)
                            ins = t.matmul(pA2[:, :],
                                           v_[:, kt * 128:(kt + 1) * 128],
                                           P2[si][:, :], start=st_, stop=sp_)
                            em.inc("pe", ins)
                            P1s[si].done("pe")
                            P2s[si].done("pe")
                        ldslot[i].done("pe")
                        em.wait(v, "pe")
                        em.dv(v.reciprocal(r1row[0:1, :], pl1[0:1, :]))
                        em.dv(v.reciprocal(r2row[0:1, :], pl2[0:1, :]))
                        pl1s.done("dv")
                        pl2s.done("dv")
                        em.dv(v.tensor_copy(r1bf[0:1, :], r1row[0:1, :]))
                        em.dv(v.tensor_copy(r2bf[0:1, :], r2row[0:1, :]))
                        em.wait(t, "dv")
                        pSs[0].begin(t)
                        pSs[1].begin(t)
                        t.matmul(pS[0][:, :], oner_sb[0:1, :], r1bf[0:1, :],
                                 start=True, stop=True)
                        ins = t.matmul(pS[1][:, :], oner_sb[0:1, :],
                                       r2bf[0:1, :], start=True, stop=True)
                        em.inc("pe", ins)
                        em.wait(v, "pe")
                        em.dv(v.tensor_copy(r1bs[:, :], pS[0][:, :]))
                        em.dv(v.tensor_copy(r2bs[:, :], pS[1][:, :]))
                        pSs[0].done("dv")
                        pSs[1].done("dv")
                        em.dv(v.tensor_mul(t1f[:, :], pA1[:, :], r1bs[:, :]))
                        em.dv(v.tensor_mul(t2f[:, :], pA2[:, :], r2bs[:, :]))
                        pA1s.done("dv")
                        pA2s.done("dv")
                        em.dv(v.scalar_tensor_tensor(tAf[:, :], t2f[:, :],
                                                     neglam[:, 0:1], t1f[:, :],
                                                     ALU.mult, ALU.add))
                        em.dv(v.tensor_copy(Abf[:, :], tAf[:, :]))
                        em.wait(a, "dv")
                        em.inc("ac", a.activation(sqA[:, :], tAf[:, :],
                                                  AF.Square))
                        em.wait(t, "ac")
                        pSs[2].begin(t)
                        pSs[3].begin(t)
                        t.matmul(pS[2][0:1, :], onec_sb[:, 0:1], Abf[:, :],
                                 start=True, stop=True)
                        ins = t.matmul(pS[3][0:1, :], onec_sb[:, 0:1], sqA[:, :],
                                       start=True, stop=True)
                        em.inc("pe", ins)
                        em.wait(v, "pe")
                        em.dv(v.tensor_scalar_mul(gmu[0:1, :], pS[2][0:1, :],
                                                  1.0 / 128))
                        em.dv(v.tensor_mul(gsc[0:1, :], gmu[0:1, :],
                                           gmu[0:1, :]))
                        em.dv(v.scalar_tensor_tensor(gsc[0:1, :], pS[3][0:1, :],
                                                     1.0 / 128, gsc[0:1, :],
                                                     ALU.mult, ALU.subtract))
                        pSs[2].done("dv")
                        pSs[3].done("dv")
                        em.wait(a, "dv")
                        em.acs(a.activation(grr[0:1, :], gsc[0:1, :], AF.Ln,
                                            bias=epsc_sb[0:1, 0:1]))
                        em.acs(a.activation(grr[0:1, :], grr[0:1, :], AF.Exp,
                                            scale=-0.5))
                        em.wait(v, "ac")
                        em.dv(v.tensor_copy(gabf[0:1, :], grr[0:1, :]))
                        em.dv(v.scalar_tensor_tensor(gbbf[0:1, :], gmu[0:1, :],
                                                     -1.0, grr[0:1, :],
                                                     ALU.mult, ALU.mult))
                        em.wait(t, "dv")
                        pSs[0].begin(t)
                        pSs[1].begin(t)
                        t.matmul(pS[0][:, :], oner_sb[0:1, :], gabf[0:1, :],
                                 start=True, stop=True)
                        ins = t.matmul(pS[1][:, :], oner_sb[0:1, :],
                                       gbbf[0:1, :], start=True, stop=True)
                        em.inc("pe", ins)
                        em.wait(v, "pe")
                        em.dv(v.tensor_copy(gab[:, :], pS[0][:, :]))
                        em.dv(v.tensor_copy(gbb[:, :], pS[1][:, :]))
                        pSs[0].done("dv")
                        pSs[1].done("dv")
                        em.dv(v.tensor_mul(tAf[:, :], tAf[:, :], gab[:, :]))
                        em.dv(v.tensor_add(tAf[:, :], tAf[:, :], gbb[:, :]))
                        afins.begin(v)
                        em.dv(v.tensor_scalar(afin[:, :], tAf[:, :],
                                              ggm_sb[:, hi:hi + 1],
                                              gbt_sb[:, hi:hi + 1],
                                              ALU.mult, ALU.add))
                        em.wait(sy, "dv")
                        em.dma(sy, attn[hi * 128:(hi + 1) * 128,
                                        qg * 512:(qg + 1) * 512],
                               afin[:, :], "sat", store=True)
                        afins.done("sat")
                nc.all_engine_barrier()
                em.final_store_waits([nc.gpsimd, nc.sync])
    return nc


def prep_B_inputs(inputs, core, q0f, k0f, v0f):
    b, hh = divmod(core, 2)
    c = host_consts()
    sl = slice(hh * 1024, (hh + 1) * 1024)
    gg = np.ascontiguousarray(
        np.asarray(inputs["L0_attn_gn_gamma"], np.float32).T[:, hh * 8:hh * 8 + 8])
    gb2 = np.ascontiguousarray(
        np.asarray(inputs["L0_attn_gn_beta"], np.float32).T[:, hh * 8:hh * 8 + 8])
    return {
        "qc": np.ascontiguousarray(q0f[b][sl, :]),
        "kc": np.ascontiguousarray(k0f[b][sl, :]),
        "vc": np.ascontiguousarray(v0f[b][:, sl]),
        "lq1": np.asarray(inputs["L0_attn_lambda_q1"],
                          np.float32).reshape(128, 1),
        "lk1": np.asarray(inputs["L0_attn_lambda_k1"],
                          np.float32).reshape(128, 1),
        "lq2": np.asarray(inputs["L0_attn_lambda_q2"],
                          np.float32).reshape(128, 1),
        "lk2": np.asarray(inputs["L0_attn_lambda_k2"],
                          np.float32).reshape(128, 1),
        "ggm": gg, "gbt": gb2,
        "masks": c["masks"], "onec": c["onec"], "oner": c["oner"],
        "epsc": np.full((128, 1), LN_EPS, np.float32),
    }



def emit_gate1(em, C, xn, T, cwT):
    nc = em.nc
    t, v, a = nc.tensor, nc.vector, nc.scalar
    ntt = T // 128
    pE, pC = C["pE"], C["pC"]
    em.wait(t, "dv")
    for k in range(ntt):
        for dc in range(NDC):
            ins = t.matmul(pE[:, k * 8:(k + 1) * 8],
                           xn[:, dc * T + k * 128:dc * T + (k + 1) * 128],
                           C["wg1T"][:, dc * 8:(dc + 1) * 8],
                           start=(dc == 0), stop=(dc == NDC - 1))
    em.inc("pe", ins)
    em.wait(a, "pe")
    em.inc("ac", a.activation(C["sg"][:, 0:ntt * 8], pE[:, 0:ntt * 8], AF.Tanh))
    em.wait(t, "ac")
    tgT = C["tgT"]
    for k in range(ntt):
        kk = k % 8
        if kk == 0:
            C["pCslot"].begin(t)
        ins = t.transpose(pC[0:8, kk * 128:(kk + 1) * 128],
                          C["sg"][:, k * 8:(k + 1) * 8], C["identf"][:, :])
        em.inc("pe", ins)
        if kk == 7 or k == ntt - 1:
            em.wait(v, "pe")
            base = (k // 8) * 1024
            n = (kk + 1) * 128
            em.dv(v.tensor_copy(tgT[0:8, base:base + n], pC[0:8, 0:n]))
            C["pCslot"].done("dv")
    em.wait(t, "dv")
    for k in range(ntt):
        ins = t.matmul(pE[:, k * 8:(k + 1) * 8], tgT[0:8, k * 128:(k + 1) * 128],
                       C["wg2T"][0:8, 0:8], start=True, stop=True)
    em.inc("pe", ins)
    cw, g1, g2, g3, m1, m2 = (C["cw"], C["g1"], C["g2"], C["g3"], C["m1"],
                              C["m2"])
    em.wait(v, "pe")
    for k in range(ntt):
        lg = C["lg"]
        em.dv(v.tensor_copy(lg[:, 0:8], pE[:, k * 8:(k + 1) * 8]))
        em.dv(v.reduce_max(m1[:, 0:1], lg[:, 0:8], axis=AXX))
        em.dv(v.tensor_scalar_mul(m2[:, 0:1], m1[:, 0:1], -1.0))
        em.wait(a, "dv")
        em.inc("ac", a.activation(C["pexp"][:, 0:8], lg[:, 0:8], AF.Exp,
                                  bias=m2[:, 0:1]))
        em.wait(v, "ac")
        em.dv(v.tensor_single_scalar(g2[:, 0:8], lg[:, 0:8], m1[:, 0:1],
                                     ALU.is_ge))
        em.dv(v.scalar_tensor_tensor(g3[:, 0:8], g2[:, 0:8], -1e30, lg[:, 0:8],
                                     ALU.mult, ALU.add))
        em.dv(v.reduce_max(m2[:, 0:1], g3[:, 0:8], axis=AXX))
        em.dv(v.tensor_single_scalar(g2[:, 0:8], lg[:, 0:8], m2[:, 0:1],
                                     ALU.is_ge))
        em.wait(v, "ac")
        em.dv(v.tensor_mul(cw[:, k * 8:(k + 1) * 8], C["pexp"][:, 0:8],
                           g2[:, 0:8]))
        em.dv(v.reduce_sum(m1[:, 0:1], cw[:, k * 8:(k + 1) * 8], axis=AXX))
        em.dv(v.reciprocal(m2[:, 0:1], m1[:, 0:1]))
        em.dv(v.tensor_single_scalar(cw[:, k * 8:(k + 1) * 8],
                                     cw[:, k * 8:(k + 1) * 8], m2[:, 0:1],
                                     ALU.mult))
    _cw_transpose(em, C, cw, ntt, cwT)


def emit_ffn1(em, C, xn, x1d, t0, T, cwT, wupT, bupc, wdownT, bdown_sb, acc,
              acc_dram):
    nc = em.nc
    t, v, a, g = nc.tensor, nc.vector, nc.scalar, nc.gpsimd
    em.sem("sacc")
    em.wait(v, "sacc")
    for s2 in ("sq20", "sq21"):
        em.sem(s2)
        em.wait(a, s2)
    nh = T // 512
    nhid = HID1 // 128
    wsA = C["wsA"]
    rot = [C["pA"], C["pB"], C["pC"]]
    rslot = [C["pAslot"], C["pBslot"], C["pCslot"]]

    em.wait(t, "dv")
    for ot in range(NDC):
        ri = ot % 2
        po, so = rot[ri], rslot[ri]
        so.begin(t)
        for h in range(nh):
            ins = t.matmul(po[:, h * 512:(h + 1) * 512],
                           bdown_sb[0:8, ot * 128:(ot + 1) * 128],
                           cwT[0:8, h * 512:(h + 1) * 512],
                           start=True, stop=True)
        em.inc("pe", ins)
        xb, xbs = C["xp"][ot % 2], C["xpslot"][ot % 2]
        xbs.begin(g)
        tok = em.dma(g, xb[:, 0:T], x1d[ot * 128:(ot + 1) * 128, t0:t0 + T],
                     f"xp{ot % 2}")
        em.waitc(v, tok)
        em.wait(v, "pe")
        em.dv(v.tensor_add(acc[:, ot * T:ot * T + T], po[:, 0:T], xb[:, 0:T]))
        xbs.done("dv")
        so.done("dv")

    for e in range(E1):
        sy = nc.sync
        em.wait(sy, "dv")
        C["cwrowslot"].begin(sy)
        rc = em.dma(sy, C["cwrow"][0:1, 0:T], cwT[e:e + 1, 0:T], "cwr")
        em.wait(t, "dv")
        em.waitc(t, rc)
        C["pCslot"].begin(t)
        for h in range(nh):
            ins = t.matmul(C["pC"][:, h * 512:(h + 1) * 512], C["oner"][0:1, :],
                           C["cwrow"][0:1, h * 512:(h + 1) * 512],
                           start=True, stop=True)
        C["cwrowslot"].done("pe")
        em.inc("pe", ins)
        em.wait(v, "pe")
        em.dv(v.tensor_copy(C["cwb"][:, 0:T], C["pC"][:, 0:T]))
        C["pCslot"].done("dv")
        for hh in range(nhid):
            bk, m = divmod(hh, WBLK // 128)
            if m == 0:
                vg, cg, sg_ = wsA.load(wupT[e][:, bk * WBLK:(bk + 1) * WBLK],
                                       NDC, WBLK)
            ia = (2 * hh) % 3
            pa, sa = rot[ia], rslot[ia]
            sa.begin(t)
            em.waitc(t, cg)
            for h in range(nh):
                for dc in range(NDC):
                    ins = t.matmul(pa[:, h * 512:(h + 1) * 512],
                                   vg(dc, m * 128, 128),
                                   xn[:, dc * T + h * 512:dc * T + (h + 1) * 512],
                                   start=(dc == 0), stop=(dc == NDC - 1))
            em.inc("pe", ins)
            sg_.done("pe")
            em.wait(a, "pe")
            C["saslot"].begin(a)
            em.inc("ac", a.activation(C["sa"][:, 0:T], pa[:, 0:T], AF.Silu,
                                      bias=bupc[:, e * 8 + hh:e * 8 + hh + 1]))
            sa.done("ac")
            em.wait(v, "ac")
            em.dv(v.tensor_mul(C["hp"][:, hh * T:hh * T + T], C["sa"][:, 0:T],
                               C["cwb"][:, 0:T]))
            C["saslot"].done("dv")
        for bk in range(D // WBLK):
            vu, cu, su_ = wsA.load(wdownT[e][:, bk * WBLK:(bk + 1) * WBLK],
                                   nhid, WBLK)
            for m in range(WBLK // 128):
                ot = bk * (WBLK // 128) + m
                ri = ot % 2
                po, so = rot[ri], rslot[ri]
                so.begin(t)
                em.waitc(t, cu)
                em.wait(t, "dv")
                for h in range(nh):
                    for hc in range(nhid):
                        io = t.matmul(
                            po[:, h * 512:(h + 1) * 512], vu(hc, m * 128, 128),
                            C["hp"][:, hc * T + h * 512:hc * T + (h + 1) * 512],
                            start=(hc == 0), stop=(hc == nhid - 1))
                em.inc("pe", io)
                su_.done("pe")
                em.wait(v, "pe")
                accm = acc[:, ot * T:ot * T + T]
                em.dv(v.tensor_add(accm, po[:, 0:T], accm))
                so.done("dv")
    em.wait(g, "dv")
    for q in range(4):
        src = acc[:, q * 4 * T:(q + 1) * 4 * T].rearrange("p (a t) -> p a t",
                                                          a=4)
        dst = acc_dram[q * 512:(q + 1) * 512, t0:t0 + T].rearrange(
            "(a p) t -> p a t", p=128)
        em.dma(g, dst, src, "sacc", store=True)


def build_C():
    nc = bass.Bass("TRN2", target_bir_lowering=False)
    ap = lambda n, shp, dt, kind: nc.dram_tensor(n, shp, dt, kind=kind).ap()
    attn0 = ap("attn0", [D, HALF], BF16, "ExternalInput")
    acc0 = ap("acc0", [D, HALF], BF16, "ExternalInput")
    woT = ap("woT", [D, D], F32, "ExternalInput")
    wq1 = ap("wq1", [D, D], F32, "ExternalInput")
    wk1 = ap("wk1", [D, D], F32, "ExternalInput")
    wv1 = ap("wv1", [D, D], F32, "ExternalInput")
    bqc = ap("bqc", [128, 16], F32, "ExternalInput")
    bkc = ap("bkc", [128, 16], F32, "ExternalInput")
    bvc = ap("bvc", [128, 16], F32, "ExternalInput")
    wg1T = ap("wg1T", [D, 8], F32, "ExternalInput")
    wg2T = ap("wg2T", [8, 8], F32, "ExternalInput")
    wupT = ap("wupT", [E1, D, HID1], F32, "ExternalInput")
    bupc = ap("bupc", [128, 64], F32, "ExternalInput")
    wdownT = ap("wdownT", [E1, HID1, D], F32, "ExternalInput")
    bdownr = ap("bdownr", [8, D], F32, "ExternalInput")
    onec = ap("onec", [128, 1], BF16, "ExternalInput")
    oner = ap("oner", [1, 128], BF16, "ExternalInput")
    identf = ap("identf", [128, 128], F32, "ExternalInput")
    identb = ap("identb", [128, 128], BF16, "ExternalInput")
    epsc = ap("epsc", [128, 1], F32, "ExternalInput")

    acc1 = ap("acc1", [D, HALF], BF16, "ExternalOutput")
    k1 = ap("k1", [D, HALF], BF16, "ExternalOutput")
    v1 = ap("v1", [HALF, D], BF16, "ExternalOutput")
    qf1 = ap("qf1", [D, HALF], BF16, "ExternalOutput")
    x1d = nc.dram_tensor("x1d", [D, HALF], BF16).ap()

    with contextlib.ExitStack() as stack:
        em = Em(nc, stack)
        sb = lambda n, shp, dt=BF16: stack.enter_context(nc.sbuf_tensor(n, shp, dt))
        psa = lambda n, shp, dt=F32: stack.enter_context(
            nc.psum_tensor(n, shp, dt))

        C = {}
        C["pA"] = psa("pA", [128, 1024])
        C["pB"] = psa("pB", [128, 1024])
        C["pC"] = psa("pC", [128, 1024])
        C["pVT"] = psa("pVT", [128, 1024], BF16)
        C["pE"] = psa("pE", [128, 512])
        for nme in ("pA", "pB", "pC", "pVT", "pE"):
            C[nme + "slot"] = Slot(em)

        xn = sb("xn", [128, NDC * TT])
        acc = sb("acc", [128, NDC * TT], F32)
        attn_sb = sb("attnsb", [128, NDC * TT])
        C["hp"] = sb("hp", [128, (HID1 // 128) * TT])
        C["xp"] = [sb("xp0", [128, TT]), sb("xp1", [128, TT])]
        C["xpslot"] = [Slot(em), Slot(em)]
        C["sq"] = sb("sq", [128, TT])
        C["sqslot"] = Slot(em)
        C["sa"] = sb("sa", [128, TT])
        C["saslot"] = Slot(em)
        C["cwb"] = sb("cwb", [128, TT])
        C["absb"] = sb("absb", [128, TT])
        C["bbsb"] = sb("bbsb", [128, TT])
        cwT = sb("cwT", [8, TT])
        C["mu"] = sb("mu", [1, TT], F32)
        C["r"] = sb("r", [1, TT], F32)
        C["abf"] = sb("abf", [1, TT])
        C["bbf"] = sb("bbf", [1, TT])
        C["sg"] = sb("sgb", [128, 64], F32)
        C["cw"] = sb("cwb2", [128, 64], F32)
        C["lg"] = sb("lg", [128, 8], F32)
        C["pexp"] = sb("pexp", [128, 8], F32)
        C["g1"] = sb("g1", [128, 8], F32)
        C["g2"] = sb("g2", [128, 8], F32)
        C["g3"] = sb("g3", [128, 8], F32)
        C["m1"] = sb("m1", [128, 1], F32)
        C["m2"] = sb("m2", [128, 1], F32)
        C["tgT"] = sb("tgT", [8, TT])
        C["onec"] = sb("onecs", [128, 1])
        C["oner"] = sb("oners", [1, 128])
        C["identf"] = sb("idf", [128, 128], F32)
        C["identb"] = sb("idb", [128, 128])
        C["epsc"] = sb("epscs", [128, 1], F32)
        C["cwrow"] = sb("cwrow", [1, TT])
        C["cwrowslot"] = Slot(em)
        C["wg1T"] = sb("wg1Ts", [128, NDC * 8])
        C["wg2T"] = sb("wg2Ts", [8, 8])
        bq_sb = sb("bqs", [128, 16], F32)
        bk_sb = sb("bks", [128, 16], F32)
        bv_sb = sb("bvs", [128, 16], F32)
        bupc_sb = sb("bupcs", [128, 64], F32)
        bdown_sb = sb("bdowns", [8, D])
        evq = sb("evq", [128, TT])
        evqs = Slot(em)
        evv = sb("evv", [128, TT])
        evvs = Slot(em)
        vtm = sb("vtm", [128, TT])
        vtms = Slot(em)
        x1c = sb("x1c", [128, TT])
        x1cs = Slot(em)

        C["wsA"] = WStream(em, "wsA", NDC * WBLK)

        with nc.Block() as block:
            @block.tensor
            def _(t):
                g, v, a, sy = nc.gpsimd, nc.vector, nc.scalar, nc.sync
                em.dma(g, C["onec"][:, :], onec, "cst")
                em.dma(g, C["oner"][0:1, :], oner, "cst")
                em.dma(g, C["identf"][:, :], identf, "cst")
                em.dma(g, C["identb"][:, :], identb, "cst")
                em.dma(g, C["epsc"][:, :], epsc, "cst")
                em.dma(g, C["wg1T"][:, :].rearrange("p (a h) -> p a h", a=NDC),
                       wg1T.rearrange("(a p) h -> p a h", p=128), "cst")
                em.dma(g, C["wg2T"][0:8, :], wg2T, "cst")
                em.dma(g, bq_sb[:, :], bqc, "cst")
                em.dma(g, bk_sb[:, :], bkc, "cst")
                em.dma(g, bv_sb[:, :], bvc, "cst")
                em.dma(g, bupc_sb[:, :], bupc, "cst")
                em.dma(g, bdown_sb[0:8, :], bdownr, "cst")
                for e_ in (t, v, a):
                    em.wait(e_, "cst")

                for tt in range(NT):
                    t0 = tt * TT
                    nc.all_engine_barrier()
                    tok = em.dma(g, attn_sb[:, :].rearrange(
                        "p (a x) -> p a x", a=NDC),
                        attn0[:, t0:t0 + TT].rearrange("(a p) x -> p a x",
                                                       p=128), "atn")
                    em.waitc(t, tok)

                    def oproj_handler(ot, pso, oslot):
                        xb, xbs = C["xp"][ot % 2], C["xpslot"][ot % 2]
                        xbs.begin(g)
                        tk = em.dma(g, xb[:, 0:TT],
                                    acc0[ot * 128:(ot + 1) * 128, t0:t0 + TT],
                                    f"xp{ot % 2}")
                        em.waitc(v, tk)
                        em.wait(v, "pe")
                        x1cs.begin(v)
                        em.dv(v.tensor_add(x1c[:, 0:TT], pso[:, 0:TT],
                                           xb[:, 0:TT]))
                        xbs.done("dv")
                        oslot.done("dv")
                        em.wait(g, "dv")
                        em.dma(g, x1d[ot * 128:(ot + 1) * 128, t0:t0 + TT],
                               x1c[:, 0:TT], "sx1")
                        x1cs.done("sx1")

                    emit_proj(em, C, C["wsA"], woT, attn_sb, TT, D,
                              oproj_handler)
                    nc.all_engine_barrier()
                    em.wait(g, "sx1")
                    emit_ln(em, C, x1d, t0, TT, xn)
                    nc.all_engine_barrier()
                    emit_gate1(em, C, xn, TT, cwT)
                    nc.all_engine_barrier()

                    def mk_handler(dst, bias_sb):
                        def h_(ot, pso, oslot):
                            em.wait(v, "pe")
                            evqs.begin(v)
                            em.dv(v.tensor_single_scalar(
                                evq[:, 0:TT], pso[:, 0:TT],
                                bias_sb[:, ot:ot + 1], ALU.add))
                            oslot.done("dv")
                            em.wait(sy, "dv")
                            em.dma(sy, dst[ot * 128:(ot + 1) * 128,
                                           t0:t0 + TT], evq[:, 0:TT],
                                   "sq1", store=True)
                            evqs.done("sq1")
                        return h_

                    emit_proj(em, C, C["wsA"], wq1, xn, TT, D,
                              mk_handler(qf1, bq_sb))
                    emit_proj(em, C, C["wsA"], wk1, xn, TT, D,
                              mk_handler(k1, bk_sb))
                    emit_proj(em, C, C["wsA"], wv1, xn, TT, D,
                              _v_handler_factory(em, C, v1, bv_sb, t0, evv,
                                                 evvs, vtm, vtms))
                    nc.all_engine_barrier()
                    emit_ffn1(em, C, xn, x1d, t0, TT, cwT,
                              [wupT[e] for e in range(E1)], bupc_sb,
                              [wdownT[e] for e in range(E1)], bdown_sb, acc,
                              acc1)
                nc.all_engine_barrier()
                em.final_store_waits([nc.gpsimd, nc.sync])
    return nc


def _v_handler_factory(em, C, v1, bv_sb, t0, evv, evvs, vtm, vtms):
    nc = em.nc
    t, v, a, sy = nc.tensor, nc.vector, nc.scalar, nc.sync

    def h_(ot, pso, oslot):
        em.wait(v, "pe")
        evvs.begin(v)
        em.dv(v.tensor_single_scalar(evv[:, 0:TT], pso[:, 0:TT],
                                     bv_sb[:, ot:ot + 1], ALU.add))
        oslot.done("dv")
        em.wait(t, "dv")
        C["pVTslot"].begin(t)
        for i in range(TT // 128):
            ins = t.transpose(C["pVT"][:, i * 128:(i + 1) * 128],
                              evv[:, i * 128:(i + 1) * 128], C["identb"][:, :])
        em.inc("pe", ins)
        evvs.done("pe")
        em.wait(v, "pe")
        vtms.begin(v)
        em.dv(v.tensor_copy(vtm[:, 0:TT], C["pVT"][:, 0:TT]))
        C["pVTslot"].done("dv")
        em.wait(sy, "dv")
        em.dma(sy, v1[t0:t0 + TT, ot * 128:(ot + 1) * 128].rearrange(
            "(i p) d -> p i d", p=128),
            vtm[:, 0:TT].rearrange("p (i d) -> p i d", d=128), "sq2",
            store=True)
        vtms.done("sq2")
    return h_


def prep_C_inputs(inputs, core, attn0f, acc0_parts):
    b, hh = divmod(core, 2)
    c = host_consts()
    sl = slice(hh * HALF, (hh + 1) * HALF)
    bup = np.asarray(inputs["L1_ffn_bup"], np.float32)
    bupc = np.concatenate([bup[e].reshape(8, 128).T for e in range(8)], axis=1)
    return {
        "attn0": np.ascontiguousarray(attn0f[b][:, sl]),
        "acc0": np.asarray(acc0_parts[core]),
        "woT": _wt(inputs["L0_attn_W_o"]),
        "wq1": np.ascontiguousarray(np.asarray(inputs["L1_attn_Wq"],
                                               np.float32)),
        "wk1": np.ascontiguousarray(np.asarray(inputs["L1_attn_Wk"],
                                               np.float32)),
        "wv1": np.ascontiguousarray(np.asarray(inputs["L1_attn_Wv"],
                                               np.float32)),
        "bqc": _bcol(inputs["L1_attn_bq"], 16),
        "bkc": _bcol(inputs["L1_attn_bk"], 16),
        "bvc": _bcol(inputs["L1_attn_bv"], 16),
        "wg1T": _wt(inputs["L1_ffn_Wg1"]),
        "wg2T": _wt(inputs["L1_ffn_Wg2"]),
        "wupT": _wt3(inputs["L1_ffn_Wup"]),
        "bupc": np.ascontiguousarray(bupc),
        "wdownT": _wt3(inputs["L1_ffn_Wdown"]),
        "bdownr": np.ascontiguousarray(np.asarray(inputs["L1_ffn_bdown"],
                                                  np.float32)),
        "onec": c["onec"], "oner": c["oner"],
        "identf": c["identf"], "identb": c["identb"],
        "epsc": np.full((128, 1), LN_EPS, np.float32),
    }



def build_D():
    nc = bass.Bass("TRN2", target_bir_lowering=False)
    ap = lambda n, shp, dt, kind: nc.dram_tensor(n, shp, dt, kind=kind).ap()
    qd = ap("qc", [8 * 128, S], BF16, "ExternalInput")
    kd = ap("kc", [8 * 128, S], BF16, "ExternalInput")
    vd = ap("vc", [S, 8 * 128], BF16, "ExternalInput")
    masks = ap("masks", [4, 128, 512], BF16, "ExternalInput")
    onec = ap("onec", [128, 1], BF16, "ExternalInput")
    oner = ap("oner", [1, 128], BF16, "ExternalInput")
    attn = ap("attn", [8 * 128, S], BF16, "ExternalOutput")
    SCALE1 = 1.0 / float(np.sqrt(128.0))

    NQG = S // 512
    with contextlib.ExitStack() as stack:
        em = Em(nc, stack)
        sb = lambda n, shp, dt=BF16: stack.enter_context(nc.sbuf_tensor(n, shp, dt))
        psa = lambda n, shp, dt=F32: stack.enter_context(nc.psum_tensor(n, shp, dt))

        pS = [psa(f"pS{i}", [128, 512]) for i in range(2)]
        pSs = [Slot(em), Slot(em)]
        pB = psa("pB", [128, 512])
        pBs = Slot(em)
        pA = psa("pA", [128, 512])
        pl = psa("pl", [128, 512])
        pAs, pls = Slot(em), Slot(em)

        qsb = [sb(f"qsb{i}", [128, S]) for i in range(2)]
        ksb = [sb(f"ksb{i}", [128, S]) for i in range(2)]
        vsb = [sb(f"vsb{i}", [128, S]) for i in range(2)]
        ldslot = [Slot(em), Slot(em)]
        P1 = [sb(f"P1{i}", [128, 512]) for i in range(2)]
        P1s = [Slot(em), Slot(em)]
        msk_sb = sb("msk", [128, 4 * 512])
        onec_sb = sb("onecs", [128, 1])
        oner_sb = sb("oners", [1, 128])
        r1row = sb("r1row", [1, 512], F32)
        r1bf = sb("r1bf", [1, 512])
        r1bs = sb("r1bs", [128, 512], F32)
        afin = sb("afin", [128, 512])
        afins = Slot(em)

        with nc.Block() as block:
            @block.tensor
            def _(t):
                g, v, a, sy = nc.gpsimd, nc.vector, nc.scalar, nc.sync
                em.dma(g, onec_sb[:, :], onec, "cst")
                em.dma(g, oner_sb[0:1, :], oner, "cst")
                em.dma(g, msk_sb[:, :].rearrange("p (a x) -> p a x", a=4),
                       masks.rearrange("a p x -> p a x"), "cst")
                for e_ in (t, v, a):
                    em.wait(e_, "cst")

                def load_head(hi):
                    i = hi % 2
                    ldslot[i].begin(g)
                    em.dma(g, qsb[i][:, :], qd[hi * 128:(hi + 1) * 128, :],
                           f"hl{i}")
                    em.dma(g, ksb[i][:, :], kd[hi * 128:(hi + 1) * 128, :],
                           f"hl{i}")
                    for part in range(4):
                        tok = em.dma(
                            g,
                            vsb[i][:, part * 1024:(part + 1) * 1024].rearrange(
                                "p (kt d) -> p kt d", d=128),
                            vd[part * 1024:(part + 1) * 1024,
                               hi * 128:(hi + 1) * 128].rearrange(
                                "(kt p) d -> p kt d", p=128), f"hl{i}")
                    return tok

                hcnt = [load_head(0)]
                for hi in range(8):
                    if hi < 7:
                        hcnt.append(load_head(hi + 1))
                    i = hi % 2
                    q_, k_, v_ = qsb[i], ksb[i], vsb[i]
                    for qg in range(NQG):
                        kts = 4 * qg + 4
                        pAs.begin(t)
                        pls.begin(t)
                        for kt in range(kts):
                            si = kt % 2
                            S1 = pS[si]
                            pSs[si].begin(t)
                            em.waitc(t, hcnt[hi])
                            ins = t.matmul(S1[:, :],
                                           k_[:, kt * 128:(kt + 1) * 128],
                                           q_[:, qg * 512:(qg + 1) * 512],
                                           start=True, stop=True)
                            em.inc("pe", ins)
                            em.wait(a, "pe")
                            P1s[si].begin(a)
                            em.inc("ac", a.activation(P1[si][:, :], S1[:, :],
                                                      AF.Exp, scale=SCALE1))
                            pSs[si].done("ac")
                            if kt >= kts - 4:
                                didx = kt - 4 * qg
                                mk = msk_sb[:, didx * 512:(didx + 1) * 512]
                                em.wait(v, "ac")
                                em.dv(v.tensor_mul(P1[si][:, :], P1[si][:, :],
                                                   mk))
                                em.wait(t, "dv")
                            else:
                                em.wait(t, "ac")
                            st_ = (kt == 0)
                            sp_ = (kt == kts - 1)
                            t.matmul(pl[0:1, :], onec_sb[:, 0:1], P1[si][:, :],
                                     start=st_, stop=sp_)
                            ins = t.matmul(pA[:, :],
                                           v_[:, kt * 128:(kt + 1) * 128],
                                           P1[si][:, :], start=st_, stop=sp_)
                            em.inc("pe", ins)
                            P1s[si].done("pe")
                        ldslot[i].done("pe")
                        em.wait(v, "pe")
                        em.dv(v.reciprocal(r1row[0:1, :], pl[0:1, :]))
                        pls.done("dv")
                        em.dv(v.tensor_copy(r1bf[0:1, :], r1row[0:1, :]))
                        em.wait(t, "dv")
                        pBs.begin(t)
                        ins = t.matmul(pB[:, :], oner_sb[0:1, :], r1bf[0:1, :],
                                       start=True, stop=True)
                        em.inc("pe", ins)
                        em.wait(v, "pe")
                        em.dv(v.tensor_copy(r1bs[:, :], pB[:, :]))
                        pBs.done("dv")
                        afins.begin(v)
                        em.dv(v.tensor_mul(afin[:, :], pA[:, :], r1bs[:, :]))
                        pAs.done("dv")
                        em.wait(sy, "dv")
                        em.dma(sy, attn[hi * 128:(hi + 1) * 128,
                                        qg * 512:(qg + 1) * 512],
                               afin[:, :], "sat", store=True)
                        afins.done("sat")
                nc.all_engine_barrier()
                em.final_store_waits([nc.gpsimd, nc.sync])
    return nc


def build_qeff(qf1_full):
    outs = []
    for i in range(16):
        qsl = np.asarray(qf1_full[:, i * 256:(i + 1) * 256])
        qe = qsl.reshape(16, 128, 256).transpose(1, 2, 0).reshape(128, 4096)
        outs.append(qe)
    return np.concatenate(outs, 0)


def prep_D_inputs(core, qef, k1f, v1f):
    b, hh = divmod(core, 2)
    c = host_consts()
    sl = slice(hh * 1024, (hh + 1) * 1024)
    return {
        "qc": np.ascontiguousarray(qef[b][sl, :]),
        "kc": np.ascontiguousarray(k1f[b][sl, :]),
        "vc": np.ascontiguousarray(v1f[b][:, sl]),
        "masks": c["masks"], "onec": c["onec"], "oner": c["oner"],
    }



def build_E():
    nc = bass.Bass("TRN2", target_bir_lowering=False)
    ap = lambda n, shp, dt, kind: nc.dram_tensor(n, shp, dt, kind=kind).ap()
    attn1 = ap("attn1", [D, HALF], BF16, "ExternalInput")
    acc1 = ap("acc1", [D, HALF], BF16, "ExternalInput")
    wo1 = ap("wo1", [D, D], F32, "ExternalInput")
    boc = ap("boc", [128, 16], F32, "ExternalInput")
    xo = ap("xo", [D, HALF], F32, "ExternalOutput")

    with contextlib.ExitStack() as stack:
        em = Em(nc, stack)
        sb = lambda n, shp, dt=BF16: stack.enter_context(nc.sbuf_tensor(n, shp, dt))
        psa = lambda n, shp, dt=F32: stack.enter_context(nc.psum_tensor(n, shp, dt))
        C = {}
        C["pA"] = psa("pA", [128, 1024])
        C["pB"] = psa("pB", [128, 1024])
        C["pAslot"], C["pBslot"] = Slot(em), Slot(em)
        attn_sb = sb("attnsb", [128, NDC * TT])
        bo_sb = sb("bos", [128, 16], F32)
        xb = [sb("xb0", [128, TT]), sb("xb1", [128, TT])]
        xbs = [Slot(em), Slot(em)]
        ot1 = sb("ot1", [128, TT], F32)
        ot1s = Slot(em)
        C["wsA"] = WStream(em, "wsA", NDC * WBLK)

        with nc.Block() as block:
            @block.tensor
            def _(t):
                g, v, a, sy = nc.gpsimd, nc.vector, nc.scalar, nc.sync
                em.dma(g, bo_sb[:, :], boc, "cst")
                for e_ in (t, v):
                    em.wait(e_, "cst")
                for tt in range(NT):
                    t0 = tt * TT
                    nc.all_engine_barrier()
                    tok = em.dma(g, attn_sb[:, :].rearrange(
                        "p (a x) -> p a x", a=NDC),
                        attn1[:, t0:t0 + TT].rearrange("(a p) x -> p a x",
                                                       p=128), "atn")
                    em.waitc(t, tok)

                    def out_handler(ot, pso, oslot):
                        xbi, xbsi = xb[ot % 2], xbs[ot % 2]
                        xbsi.begin(g)
                        tk = em.dma(g, xbi[:, 0:TT],
                                    acc1[ot * 128:(ot + 1) * 128, t0:t0 + TT],
                                    f"xp{ot % 2}")
                        em.waitc(v, tk)
                        em.wait(v, "pe")
                        ot1s.begin(v)
                        em.dv(v.tensor_single_scalar(ot1[:, 0:TT], pso[:, 0:TT],
                                                     bo_sb[:, ot:ot + 1],
                                                     ALU.add))
                        oslot.done("dv")
                        em.dv(v.tensor_add(ot1[:, 0:TT], ot1[:, 0:TT],
                                           xbi[:, 0:TT]))
                        xbsi.done("dv")
                        em.wait(sy, "dv")
                        em.dma(sy, xo[ot * 128:(ot + 1) * 128, t0:t0 + TT],
                               ot1[:, 0:TT], "sxo", store=True)
                        ot1s.done("sxo")

                    emit_proj(em, C, C["wsA"], wo1, attn_sb, TT, D, out_handler)
                nc.all_engine_barrier()
                em.final_store_waits([nc.gpsimd, nc.sync])
    return nc


def prep_E_inputs(inputs, core, attn1f, acc1_parts):
    b, hh = divmod(core, 2)
    sl = slice(hh * HALF, (hh + 1) * HALF)
    return {
        "attn1": np.ascontiguousarray(attn1f[b][:, sl]),
        "acc1": np.asarray(acc1_parts[core]),
        "wo1": np.ascontiguousarray(np.asarray(inputs["L1_attn_Wo"],
                                               np.float32)),
        "boc": _bcol(inputs["L1_attn_bo"], 16),
    }



def kernel_v0(**inputs):
    cores = list(range(8))
    ncA = build_A()
    resA = run_bass_kernel_spmd(ncA, [prep_A_inputs(inputs, c) for c in cores],
                                core_ids=cores)
    ra = resA.results
    q0f = [np.concatenate([np.asarray(ra[2 * b]["q0"]),
                           np.asarray(ra[2 * b + 1]["q0"])], 1) for b in range(4)]
    k0f = [np.concatenate([np.asarray(ra[2 * b]["k0"]),
                           np.asarray(ra[2 * b + 1]["k0"])], 1) for b in range(4)]
    v0f = [np.concatenate([np.asarray(ra[2 * b]["v0"]),
                           np.asarray(ra[2 * b + 1]["v0"])], 0) for b in range(4)]
    ncB = build_B()
    resB = run_bass_kernel_spmd(
        ncB, [prep_B_inputs(inputs, c, q0f, k0f, v0f) for c in cores],
        core_ids=cores)
    rb = resB.results
    attn0f = [np.concatenate([np.asarray(rb[2 * b]["attn"]),
                              np.asarray(rb[2 * b + 1]["attn"])], 0)
              for b in range(4)]
    acc0_parts = [np.asarray(ra[c]["acc0"]) for c in cores]
    ncC = build_C()
    resC = run_bass_kernel_spmd(
        ncC, [prep_C_inputs(inputs, c, attn0f, acc0_parts) for c in cores],
        core_ids=cores)
    rc = resC.results
    k1f = [np.concatenate([np.asarray(rc[2 * b]["k1"]),
                           np.asarray(rc[2 * b + 1]["k1"])], 1) for b in range(4)]
    v1f = [np.concatenate([np.asarray(rc[2 * b]["v1"]),
                           np.asarray(rc[2 * b + 1]["v1"])], 0) for b in range(4)]
    qef = [build_qeff(np.concatenate([np.asarray(rc[2 * b]["qf1"]),
                                      np.asarray(rc[2 * b + 1]["qf1"])], 1))
           for b in range(4)]
    ncD = build_D()
    resD = run_bass_kernel_spmd(
        ncD, [prep_D_inputs(c, qef, k1f, v1f) for c in cores], core_ids=cores)
    rd = resD.results
    attn1f = [np.concatenate([np.asarray(rd[2 * b]["attn"]),
                              np.asarray(rd[2 * b + 1]["attn"])], 0)
              for b in range(4)]
    acc1_parts = [np.asarray(rc[c]["acc1"]) for c in cores]
    ncE = build_E()
    resE = run_bass_kernel_spmd(
        ncE, [prep_E_inputs(inputs, c, attn1f, acc1_parts) for c in cores],
        core_ids=cores)
    re = resE.results
    out = np.empty((4, 4096, 2048), np.float32)
    for c in cores:
        b, hh = divmod(c, 2)
        out[b, hh * HALF:(hh + 1) * HALF, :] = np.asarray(re[c]["xo"]).T
    return out


def switch2(em, engs, idx, body_fn):
    nc = em.nc
    st0 = em.snap_all()
    deltas = []
    for h in nc.Switch(engines=engs, index=idx, n=2):
        body_fn(h)
        deltas.append(em.snap_all())
        if h == 0:
            em.restore_all(st0)
    assert deltas[0][0] == deltas[1][0], ("switch bodies diverge",
                                          deltas[0][0], deltas[1][0])



NTT_F = S // TT


def emit_attn0_phase(em, B, qd, kd, vd, attn_out):
    nc = em.nc
    t, v, a, g, sy = nc.tensor, nc.vector, nc.scalar, nc.gpsimd, nc.sync
    NQG = S // 512
    pA, pB, pC, pD = B["pApair"], B["pBpair"], B["pC"], B["pD"]
    sslots = B["pSs"]
    pA1s, pl1s = B["pA1s"], B["pl1s"]
    qsb, ksb, vsb, ldslot = B["qsb"], B["ksb"], B["vsb"], B["ldslot"]
    P12, P12s = B["P12"], B["P12s"]
    msk_sb = B["msk"]

    def load_head(hi):
        i = hi % 2
        ldslot[i].begin(g)
        em.dma(g, qsb[i][:, :], qd[hi * 128:(hi + 1) * 128, :], f"hl{i}")
        em.dma(g, ksb[i][:, :], kd[hi * 128:(hi + 1) * 128, :], f"hl{i}")
        for part in range(4):
            tok = em.dma(
                g, vsb[i][:, part * 1024:(part + 1) * 1024].rearrange(
                    "p (kt d) -> p kt d", d=128),
                vd[part * 1024:(part + 1) * 1024,
                   hi * 128:(hi + 1) * 128].rearrange(
                    "(kt p) d -> p kt d", p=128), f"hl{i}")
        return tok

    def kts_loop(hi, qg):
        i = hi % 2
        q_, k_, v_ = qsb[i], ksb[i], vsb[i]
        kts = 4 * qg + 4
        pA1s.begin(t)
        pl1s.begin(t)
        pend = None

        def consume(si, tok, kt):
            t.wait_ge(em.sems[tok[0]], tok[1])
            st_, sp_ = (kt == 0), (kt == kts - 1)
            P1 = P12[si][:, 0:512]
            P2 = P12[si][:, 512:1024]
            t.matmul(pD[0:1, 0:512], B["onec"][:, 0:1], P1, start=st_,
                     stop=sp_)
            t.matmul(pD[0:1, 512:1024], B["onec"][:, 0:1], P2, start=st_,
                     stop=sp_)
            t.matmul(pC[:, 0:512], v_[:, kt * 128:(kt + 1) * 128], P1,
                     start=st_, stop=sp_)
            ins = t.matmul(pC[:, 512:1024], v_[:, kt * 128:(kt + 1) * 128],
                           P2, start=st_, stop=sp_)
            em.inc("pe", ins)
            P12s[si].done("pe")

        for kt in range(kts):
            si = kt % 2
            Sp = pA if si == 0 else pB
            sslots[si].begin(t)
            em.waitc(t, hcnt[hi])
            t.matmul(Sp[:, 0:512], k_[0:64, kt * 128:(kt + 1) * 128],
                     q_[0:64, qg * 512:(qg + 1) * 512], start=True, stop=True)
            ins = t.matmul(Sp[:, 512:1024],
                           k_[64:128, kt * 128:(kt + 1) * 128],
                           q_[64:128, qg * 512:(qg + 1) * 512],
                           start=True, stop=True)
            em.inc("pe", ins)
            em.wait(a, "pe")
            P12s[si].begin(a)
            em.inc("ac", a.activation(P12[si][:, :], Sp[:, :], AF.Exp,
                                      scale=0.125))
            sslots[si].done("ac")
            if kt >= kts - 4:
                didx = kt - 4 * qg
                em.wait(v, "ac")
                em.inc("dv", v.tensor_mul(P12[si][:, :], P12[si][:, :],
                                          msk_sb[:, didx * 1024:
                                                 (didx + 1) * 1024]))
                v.wait_ge(em.sems["dv"], em.cnt["dv"])
                tok = ("dv", em.cnt["dv"])
            else:
                tok = ("ac", em.cnt["ac"])
            if pend is not None:
                consume(*pend)
            pend = (si, tok, kt)
        consume(*pend)

    def stage(par):
        em.wait(v, "pe")
        em.dv(v.reciprocal(B["r1row"][par][0:1, :], pD[0:1, 0:512]))
        em.dv(v.reciprocal(B["r2row"][par][0:1, :], pD[0:1, 512:1024]))
        pl1s.done("dv")
        em.dv(v.tensor_copy(B["sA1"][par][:, :], pC[:, 0:512]))
        em.dv(v.tensor_copy(B["sA2"][par][:, :], pC[:, 512:1024]))
        pA1s.done("dv")
        em.dv(v.tensor_copy(B["r1bf"][par][0:1, :], B["r1row"][par][0:1, :]))
        em.dv(v.tensor_copy(B["r2bf"][par][0:1, :], B["r2row"][par][0:1, :]))

    def epi_rest(hi, qg, par):
        bc1, bc2 = pA[:, 0:512], pA[:, 512:1024]
        em.wait(t, "dv")
        sslots[0].begin(t)
        t.matmul(bc1, B["oner"][0:1, :], B["r1bf"][par][0:1, :], start=True,
                 stop=True)
        ins = t.matmul(bc2, B["oner"][0:1, :], B["r2bf"][par][0:1, :],
                       start=True, stop=True)
        em.inc("pe", ins)
        em.wait(v, "pe")
        em.dv(v.tensor_mul(B["t1f"][:, :], B["sA1"][par][:, :], bc1))
        em.dv(v.tensor_mul(B["t2f"][:, :], B["sA2"][par][:, :], bc2))
        sslots[0].done("dv")
        em.dv(v.scalar_tensor_tensor(B["tAf"][:, :], B["t2f"][:, :],
                                     B["neglam"][:, 0:1], B["t1f"][:, :],
                                     ALU.mult, ALU.add))
        em.dv(v.tensor_copy(B["Abf"][:, :], B["tAf"][:, :]))
        em.wait(a, "dv")
        em.inc("ac", a.activation(B["sqA"][:, :], B["tAf"][:, :], AF.Square))
        em.wait(t, "ac")
        sslots[1].begin(t)
        t.matmul(pB[0:1, 0:512], B["onec"][:, 0:1], B["Abf"][:, :],
                 start=True, stop=True)
        ins = t.matmul(pB[0:1, 512:1024], B["onec"][:, 0:1], B["sqA"][:, :],
                       start=True, stop=True)
        em.inc("pe", ins)
        em.wait(v, "pe")
        em.dv(v.tensor_scalar_mul(B["gmu"][0:1, :], pB[0:1, 0:512], 1.0 / 128))
        em.dv(v.tensor_mul(B["gsc"][0:1, :], B["gmu"][0:1, :],
                           B["gmu"][0:1, :]))
        em.dv(v.scalar_tensor_tensor(B["gsc"][0:1, :], pB[0:1, 512:1024],
                                     1.0 / 128, B["gsc"][0:1, :], ALU.mult,
                                     ALU.subtract))
        sslots[1].done("dv")
        em.wait(a, "dv")
        em.acs(a.activation(B["grr"][0:1, :], B["gsc"][0:1, :], AF.Ln,
                            bias=B["epsc"][0:1, 0:1]))
        em.acs(a.activation(B["grr"][0:1, :], B["grr"][0:1, :], AF.Exp,
                            scale=-0.5))
        em.wait(v, "ac")
        em.dv(v.tensor_copy(B["gabf"][0:1, :], B["grr"][0:1, :]))
        em.dv(v.scalar_tensor_tensor(B["gbbf"][0:1, :], B["gmu"][0:1, :],
                                     -1.0, B["grr"][0:1, :], ALU.mult,
                                     ALU.mult))
        em.wait(t, "dv")
        sslots[0].begin(t)
        t.matmul(bc1, B["oner"][0:1, :], B["gabf"][0:1, :], start=True,
                 stop=True)
        ins = t.matmul(bc2, B["oner"][0:1, :], B["gbbf"][0:1, :], start=True,
                       stop=True)
        em.inc("pe", ins)
        em.wait(v, "pe")
        em.dv(v.tensor_mul(B["tAf"][:, :], B["tAf"][:, :], bc1))
        em.dv(v.tensor_add(B["tAf"][:, :], B["tAf"][:, :], bc2))
        sslots[0].done("dv")
        B["afins"].begin(v)
        em.dv(v.tensor_scalar(B["afin"][:, :], B["tAf"][:, :],
                              B["ggm"][:, hi:hi + 1], B["gbt"][:, hi:hi + 1],
                              ALU.mult, ALU.add))
        em.wait(sy, "dv")
        em.dma(sy, attn_out[hi * 128:(hi + 1) * 128,
                            qg * 512:(qg + 1) * 512], B["afin"][:, :], "sat",
               store=True)
        B["afins"].done("sat")

    hcnt = [load_head(0)]
    pend_epi = None
    k = 0
    for hi in range(NH):
        if hi < NH - 1:
            hcnt.append(load_head(hi + 1))
        for qg in range(NQG):
            kts_loop(hi, qg)
            ldslot[hi % 2].done("pe")
            stage(k % 2)
            if pend_epi is not None:
                epi_rest(*pend_epi)
            pend_epi = (hi, qg, k % 2)
            k += 1
    epi_rest(*pend_epi)


def emit_attn1_phase(em, B, qf1d, kd, vd, attn_out, qgs=None):
    nc = em.nc
    t, v, a, g, sy = nc.tensor, nc.vector, nc.scalar, nc.gpsimd, nc.sync
    NQG = S // 512
    SCALE1 = 1.0 / float(np.sqrt(128.0))
    pA, pC, pD = B["pApair"], B["pC"], B["pD"]
    sviews = [pA[:, 0:512], pA[:, 512:1024]]
    sslots = B["pSs"]
    pA1s, pl1s = B["pA1s"], B["pl1s"]
    qsb, ksb, vsb, ldslot = B["qsb"], B["ksb"], B["vsb"], B["ldslot"]
    qtmp = B["qtmp"]
    P12, P12s = B["P12"], B["P12s"]
    msk_sb = B["msk"]

    def load_head(hi):
        i = hi % 2
        ldslot[i].begin(g)
        B["qtmps"].begin(g)
        tq = em.dma(g, qtmp[:, :].rearrange("p (m x) -> p m x", m=16),
                    qf1d[:, hi * 256:(hi + 1) * 256].rearrange(
                        "(m p) x -> p m x", p=128), f"hq{i}")
        em.waitc(v, tq)
        qv = qsb[i][:, :].rearrange("p (a m) -> p m a", m=16)
        for m in range(16):
            em.dv(v.tensor_copy(qv[:, m, :], qtmp[:, m * 256:(m + 1) * 256]))
        B["qtmps"].done("dv")
        ldslot[i].done("dv")
        em.dma(g, ksb[i][:, :], kd[hi * 128:(hi + 1) * 128, :], f"hl{i}")
        for part in range(4):
            tok = em.dma(
                g, vsb[i][:, part * 1024:(part + 1) * 1024].rearrange(
                    "p (kt d) -> p kt d", d=128),
                vd[part * 1024:(part + 1) * 1024,
                   hi * 128:(hi + 1) * 128].rearrange(
                    "(kt p) d -> p kt d", p=128), f"hl{i}")
        return tok

    def kts_loop(hi, qg):
        i = hi % 2
        q_, k_, v_ = qsb[i], ksb[i], vsb[i]
        kts = 4 * qg + 4
        pA1s.begin(t)
        pl1s.begin(t)
        pend = None

        def consume(si, tok, kt):
            t.wait_ge(em.sems[tok[0]], tok[1])
            st_, sp_ = (kt == 0), (kt == kts - 1)
            P1 = P12[si][:, 0:512]
            t.matmul(pD[0:1, 0:512], B["onec"][:, 0:1], P1, start=st_,
                     stop=sp_)
            ins = t.matmul(pC[:, 0:512], v_[:, kt * 128:(kt + 1) * 128], P1,
                           start=st_, stop=sp_)
            em.inc("pe", ins)
            P12s[si].done("pe")

        for kt in range(kts):
            si = kt % 2
            Sp = sviews[si]
            sslots[si].begin(t)
            em.waitc(t, hcnt[hi])
            em.wait(t, "dv")
            ins = t.matmul(Sp, k_[:, kt * 128:(kt + 1) * 128],
                           q_[:, qg * 512:(qg + 1) * 512], start=True,
                           stop=True)
            em.inc("pe", ins)
            em.wait(a, "pe")
            P12s[si].begin(a)
            em.inc("ac", a.activation(P12[si][:, 0:512], Sp, AF.Exp,
                                      scale=SCALE1))
            sslots[si].done("ac")
            if kt >= kts - 4:
                didx = kt - 4 * qg
                em.wait(v, "ac")
                em.inc("dv", v.tensor_mul(P12[si][:, 0:512],
                                          P12[si][:, 0:512],
                                          msk_sb[:, didx * 1024:
                                                 didx * 1024 + 512]))
                v.wait_ge(em.sems["dv"], em.cnt["dv"])
                tok = ("dv", em.cnt["dv"])
            else:
                tok = ("ac", em.cnt["ac"])
            if pend is not None:
                consume(*pend)
            pend = (si, tok, kt)
        consume(*pend)

    def stage(par):
        em.wait(v, "pe")
        em.dv(v.reciprocal(B["r1row"][par][0:1, :], pD[0:1, 0:512]))
        pl1s.done("dv")
        em.dv(v.tensor_copy(B["sA1"][par][:, :], pC[:, 0:512]))
        pA1s.done("dv")
        em.dv(v.tensor_copy(B["r1bf"][par][0:1, :], B["r1row"][par][0:1, :]))

    def epi_rest(hi, qg, par):
        bc1 = pA[:, 0:512]
        em.wait(t, "dv")
        sslots[0].begin(t)
        ins = t.matmul(bc1, B["oner"][0:1, :], B["r1bf"][par][0:1, :],
                       start=True, stop=True)
        em.inc("pe", ins)
        em.wait(v, "pe")
        B["afins"].begin(v)
        em.dv(v.tensor_mul(B["afin"][:, :], B["sA1"][par][:, :], bc1))
        sslots[0].done("dv")
        em.wait(sy, "dv")
        em.dma(sy, attn_out[hi * 128:(hi + 1) * 128,
                            qg * 512:(qg + 1) * 512], B["afin"][:, :], "sat",
               store=True)
        B["afins"].done("sat")

    if qgs is None:
        qgs = list(range(NQG))
    hcnt = [load_head(0)]
    pend_epi = None
    k = 0
    for hi in range(NH):
        if hi < NH - 1:
            hcnt.append(load_head(hi + 1))
        for qg in qgs:
            kts_loop(hi, qg)
            ldslot[hi % 2].done("pe")
            stage(k % 2)
            if pend_epi is not None:
                epi_rest(*pend_epi)
            pend_epi = (hi, qg, k % 2)
            k += 1
    epi_rest(*pend_epi)


def emit_ffn1h(em, C, xn, x1d, t0, coff, cwT, wupT, bupc, wdownT, bdown_sb,
               acc, acc_dram):
    nc = em.nc
    t, v, a, g = nc.tensor, nc.vector, nc.scalar, nc.gpsimd
    em.sem("sacc")
    em.wait(v, "sacc")
    for s2 in ("sq20", "sq21"):
        em.sem(s2)
        em.wait(a, s2)
    W = 512
    nhid = HID1 // 128
    wsA = C["wsA"]
    rot = [C["pA"], C["pB"], C["pC"]]
    rslot = [C["pAslot"], C["pBslot"], C["pCslot"]]

    def xv(dc):
        return xn[:, dc * TT + coff:dc * TT + coff + W]

    em.wait(t, "dv")
    for ot in range(NDC):
        ri = ot % 2
        po, so = rot[ri], rslot[ri]
        so.begin(t)
        ins = t.matmul(po[:, 0:W], bdown_sb[0:8, ot * 128:(ot + 1) * 128],
                       cwT[0:8, coff:coff + W], start=True, stop=True)
        em.inc("pe", ins)
        xb, xbs = C["xp"][ot % 2], C["xpslot"][ot % 2]
        xbs.begin(g)
        tok = em.dma(g, xb[:, 0:W],
                     x1d[ot * 128:(ot + 1) * 128, t0 + coff:t0 + coff + W],
                     f"xp{ot % 2}")
        em.waitc(v, tok)
        em.wait(v, "pe")
        em.dv(v.tensor_add(acc[:, ot * W:(ot + 1) * W], po[:, 0:W],
                           xb[:, 0:W]))
        xbs.done("dv")
        so.done("dv")

    for e in range(E1):
        sy = nc.sync
        em.wait(sy, "dv")
        C["cwrowslot"].begin(sy)
        rc = em.dma(sy, C["cwrow"][0:1, 0:W], cwT[e:e + 1, coff:coff + W],
                    "cwr")
        em.wait(t, "dv")
        em.waitc(t, rc)
        C["pCslot"].begin(t)
        ins = t.matmul(C["pC"][:, 0:W], C["oner"][0:1, :],
                       C["cwrow"][0:1, 0:W], start=True, stop=True)
        C["cwrowslot"].done("pe")
        em.inc("pe", ins)
        em.wait(v, "pe")
        em.dv(v.tensor_copy(C["cwb"][:, 0:W], C["pC"][:, 0:W]))
        C["pCslot"].done("dv")
        for hh in range(nhid):
            bk, m = divmod(hh, WBLK // 128)
            if m == 0:
                vg, cg, sg_ = wsA.load(wupT[e][:, bk * WBLK:(bk + 1) * WBLK],
                                       NDC, WBLK)
            ia = (2 * hh) % 3
            pa, sa = rot[ia], rslot[ia]
            sa.begin(t)
            em.waitc(t, cg)
            for dc in range(NDC):
                ins = t.matmul(pa[:, 0:W], vg(dc, m * 128, 128), xv(dc),
                               start=(dc == 0), stop=(dc == NDC - 1))
            em.inc("pe", ins)
            sg_.done("pe")
            em.wait(a, "pe")
            C["saslot"].begin(a)
            em.inc("ac", a.activation(C["sa"][:, 0:W], pa[:, 0:W], AF.Silu,
                                      bias=bupc[:, e * 8 + hh:e * 8 + hh + 1]))
            sa.done("ac")
            em.wait(v, "ac")
            em.dv(v.tensor_mul(C["hp"][:, hh * W:(hh + 1) * W],
                               C["sa"][:, 0:W], C["cwb"][:, 0:W]))
            C["saslot"].done("dv")
        for bk in range(D // WBLK):
            vu, cu, su_ = wsA.load(wdownT[e][:, bk * WBLK:(bk + 1) * WBLK],
                                   nhid, WBLK)
            for m in range(WBLK // 128):
                ot = bk * (WBLK // 128) + m
                ri = ot % 2
                po, so = rot[ri], rslot[ri]
                so.begin(t)
                em.waitc(t, cu)
                em.wait(t, "dv")
                for hc in range(nhid):
                    io = t.matmul(po[:, 0:W], vu(hc, m * 128, 128),
                                  C["hp"][:, hc * W:(hc + 1) * W],
                                  start=(hc == 0), stop=(hc == nhid - 1))
                em.inc("pe", io)
                su_.done("pe")
                em.wait(v, "pe")
                accm = acc[:, ot * W:(ot + 1) * W]
                em.dv(v.tensor_add(accm, po[:, 0:W], accm))
                so.done("dv")
    em.wait(g, "dv")
    for q in range(4):
        srcv = acc[:, q * 4 * W:(q + 1) * 4 * W].rearrange(
            "p (a t) -> p a t", a=4)
        dst = acc_dram[q * 512:(q + 1) * 512,
                       t0 + coff:t0 + coff + W].rearrange(
            "(a p) t -> p a t", p=128)
        em.dma(g, dst, srcv, "sacc", store=True)


def build_fused():
    nc = bass.Bass("TRN2", target_bir_lowering=False)
    ap = lambda n, shp, dt, kind: nc.dram_tensor(n, shp, dt, kind=kind).ap()
    ian = lambda n, shp, dt: nc.dram_tensor(n, shp, dt).ap()
    x0 = ap("x0", [D, S], F32, "ExternalInput")
    wqkvT = ap("wqkvT", [D, 3 * D], F32, "ExternalInput")
    embT = ap("embT", [D, 8], F32, "ExternalInput")
    gbi = ap("gb", [128, 8], F32, "ExternalInput")
    wgT = ap("wgT", [E0, D, HID0], F32, "ExternalInput")
    wdT = ap("wdT", [E0, D, HID0], F32, "ExternalInput")
    wuT = ap("wuT", [E0, HID0, D], F32, "ExternalInput")
    swgT = ap("swgT", [D, HID0], F32, "ExternalInput")
    swdT = ap("swdT", [D, HID0], F32, "ExternalInput")
    swuT = ap("swuT", [HID0, D], F32, "ExternalInput")
    lq1 = ap("lq1", [128, 1], F32, "ExternalInput")
    lk1 = ap("lk1", [128, 1], F32, "ExternalInput")
    lq2 = ap("lq2", [128, 1], F32, "ExternalInput")
    lk2 = ap("lk2", [128, 1], F32, "ExternalInput")
    ggmi = ap("ggm", [128, NH], F32, "ExternalInput")
    gbti = ap("gbt", [128, NH], F32, "ExternalInput")
    masksi = ap("masks", [4, 128, 1024], BF16, "ExternalInput")
    woT = ap("woT", [D, D], F32, "ExternalInput")
    wq1 = ap("wq1", [D, D], F32, "ExternalInput")
    wk1 = ap("wk1", [D, D], F32, "ExternalInput")
    wv1 = ap("wv1", [D, D], F32, "ExternalInput")
    bqc = ap("bqc", [128, 16], F32, "ExternalInput")
    bkc = ap("bkc", [128, 16], F32, "ExternalInput")
    bvc = ap("bvc", [128, 16], F32, "ExternalInput")
    wg1Ti = ap("wg1T", [D, 8], F32, "ExternalInput")
    wg2Ti = ap("wg2T", [8, 8], F32, "ExternalInput")
    wupT = ap("wupT", [E1, D, HID1], F32, "ExternalInput")
    bupci = ap("bupc", [128, 64], F32, "ExternalInput")
    wdownT = ap("wdownT", [E1, HID1, D], F32, "ExternalInput")
    bdownr = ap("bdownr", [8, D], F32, "ExternalInput")
    wo1 = ap("wo1", [D, D], F32, "ExternalInput")
    boc = ap("boc", [128, 16], F32, "ExternalInput")
    onec = ap("onec", [128, 1], BF16, "ExternalInput")
    oner = ap("oner", [1, 128], BF16, "ExternalInput")
    identf = ap("identf", [128, 128], F32, "ExternalInput")
    epsc = ap("epsc", [128, 1], F32, "ExternalInput")
    xo = ap("xo", [D, S], F32, "ExternalOutput")
    q0d = ian("q0d", [D, S], BF16)
    k0d = ian("k0d", [D, S], BF16)
    v0d = ian("v0d", [S, D], BF16)
    acc0d = ian("acc0d", [D, S], BF16)
    attn0d = ian("attn0d", [D, S], BF16)
    x1d = ian("x1d", [D, S], BF16)
    qf1d = ian("qf1d", [D, S], BF16)
    k1d = ian("k1d", [D, S], BF16)
    v1d = ian("v1d", [S, D], BF16)
    acc1d = ian("acc1d", [D, S], BF16)
    attn1d = ian("attn1d", [D, S], BF16)

    with contextlib.ExitStack() as stack:
        em = Em(nc, stack)
        sb = lambda n, shp, dt=BF16: stack.enter_context(nc.sbuf_tensor(n, shp, dt))
        psa = lambda n, shp, dt=F32: stack.enter_context(nc.psum_tensor(n, shp, dt))

        abf = sb("abf", [128, 40960])
        af32 = sb("af32", [128, 16896], F32)
        pA = psa("pA", [128, 1024])
        pB = psa("pB", [128, 1024])
        pC = psa("pC", [128, 1024])
        pD = psa("pD", [128, 1024])

        xp0 = sb("xp0", [128, TT])
        xp1 = sb("xp1", [128, TT])
        sq_g = sb("sqg", [128, TT])
        sa_g = sb("sag", [128, TT])
        tmp_g = sb("tmpg", [128, TT])
        cwb_g = sb("cwbg", [128, TT])
        absb_g = sb("absbg", [128, TT])
        bbsb_g = sb("bbsbg", [128, TT])
        cwT_g = sb("cwTg", [8, TT])
        evq_g = sb("evqg", [128, TT])
        vtm_g = sb("vtmg", [128, TT])
        x1c_g = sb("x1cg", [128, TT])
        sg_g = sb("sgg", [128, 64], F32)
        cw_g = sb("cwg", [128, 64], F32)
        lg_g = sb("lgg", [128, 8], F32)
        pexp_g = sb("pexpg", [128, 8], F32)
        g1_g = sb("g1g", [128, 8], F32)
        g2_g = sb("g2g", [128, 8], F32)
        g3_g = sb("g3g", [128, 8], F32)
        m1_g = sb("m1g", [128, 1], F32)
        m2_g = sb("m2g", [128, 1], F32)
        onec_g = sb("onecg", [128, 1])
        oner_g = sb("onerg", [1, 128])
        identf_g = sb("idfg", [128, 128], F32)
        epsc_g = sb("epscg", [128, 1], F32)
        embT_g = sb("embTg", [128, NDC * 8])
        gb_g = sb("gbg", [128, 8], F32)
        wg1T_g = sb("wg1Tg", [128, NDC * 8])
        wg2T_g = sb("wg2Tg", [8, 8])
        bq_g = sb("bqg", [128, 16], F32)
        bk_g = sb("bkg", [128, 16], F32)
        bv_g = sb("bvg", [128, 16], F32)
        bo_g = sb("bog", [128, 16], F32)
        bupc_g = sb("bupcg", [128, 64], F32)
        ggm_g = sb("ggmg", [128, NH], F32)
        gbt_g = sb("gbtg", [128, NH], F32)
        lam_g = [sb(f"lamg{i}", [128, 1], F32) for i in range(4)]
        neglam_g = sb("neglamg", [128, 1], F32)
        lsc1_g = sb("lsc1g", [128, 1], F32)
        lsc2_g = sb("lsc2g", [128, 1], F32)

        wsA = WStream(em, "wsA", NDC * WBLK)
        wsB = WStream(em, "wsB", NDC * WBLK)

        def mk_tok_C(hid):
            C = {}
            C["pA"], C["pB"], C["pC"] = pA, pB, pC
            C["pE"] = pD[:, 0:512]
            C["pD"] = pD
            for nme in ("pA", "pB", "pC", "pD"):
                C[nme + "slot"] = Slot(em)
            C["pEslot"] = C["pDslot"]
            C["xn"] = abf[:, 0:16384]
            nh_ = hid // 128
            C["hp"] = abf[:, 16384:16384 + nh_ * TT]
            C["attn_sb"] = abf[:, 24576:40960]
            C["bdown_sb"] = abf[0:8, 26624:28672]
            C["acc"] = af32[:, 0:16384]
            C["mu"] = af32[0:1, 0:TT]
            C["r"] = af32[0:1, TT:2 * TT]
            C["evv"] = af32[:, 0:TT]
            C["abf"] = abf[0:1, 16384:16384 + TT]
            C["bbf"] = abf[0:1, 16384 + TT:16384 + 2 * TT]
            C["xp"] = [xp0, xp1]
            C["xpslot"] = [Slot(em), Slot(em)]
            C["sq"] = sq_g
            C["sqslot"] = Slot(em)
            C["sa"] = sa_g
            C["saslot"] = Slot(em)
            C["tmp"] = tmp_g
            C["cwb"] = cwb_g
            C["absb"] = absb_g
            C["bbsb"] = bbsb_g
            C["cwrow"] = abf[0:1, 28672:28672 + TT]
            C["cwrowslot"] = Slot(em)
            C["tgT"] = abf[0:8, 29696:29696 + TT]
            C["sg"] = sg_g
            C["cw"] = cw_g
            C["lg"] = lg_g
            C["pexp"] = pexp_g
            C["g1"], C["g2"], C["g3"] = g1_g, g2_g, g3_g
            C["m1"], C["m2"] = m1_g, m2_g
            C["onec"], C["oner"] = onec_g, oner_g
            C["identf"] = identf_g
            C["epsc"] = epsc_g
            C["embT"] = embT_g
            C["gb"] = gb_g
            C["wg1T"] = wg1T_g
            C["wg2T"] = wg2T_g
            C["wsA"], C["wsB"] = wsA, wsB
            C["evq2"] = [evq_g, x1c_g]
            C["evqs2"] = [Slot(em), Slot(em)]
            C["evv2"] = [af32[:, 0:TT], af32[:, TT:2 * TT]]
            C["evvs2"] = [Slot(em), Slot(em)]
            C["vtm2"] = [vtm_g, sa_g]
            C["vtms2"] = [Slot(em), Slot(em)]
            C["x1cs"] = Slot(em)
            C["ot1s"] = Slot(em)
            C["asbslot"] = Slot(em)
            return C

        CA = mk_tok_C(HID0)
        CC = mk_tok_C(HID1)

        def mk_attn_B():
            B = {}
            B["pApair"], B["pBpair"] = pA, pB
            B["pC"], B["pD"] = pC, pD
            B["pSs"] = [Slot(em), Slot(em)]
            B["pA1s"], B["pl1s"] = Slot(em), Slot(em)
            B["qsb"] = [abf[:, 0:4096], abf[:, 4096:8192]]
            B["ksb"] = [abf[:, 8192:12288], abf[:, 12288:16384]]
            B["vsb"] = [abf[:, 16384:20480], abf[:, 20480:24576]]
            B["ldslot"] = [Slot(em), Slot(em)]
            B["qtmp"] = abf[:, 24576:28672]
            B["qtmps"] = Slot(em)
            B["msk"] = abf[:, 28672:32768]
            B["P12"] = [abf[:, 32768:33792], abf[:, 33792:34816]]
            B["P12s"] = [Slot(em), Slot(em)]
            B["afin"] = abf[:, 34816:35328]
            B["afins"] = Slot(em)
            B["Abf"] = abf[:, 35328:35840]
            B["sqA"] = abf[:, 35840:36352]
            B["r1bf"] = [abf[0:1, 36352:36864], abf[0:1, 37376:37888]]
            B["r2bf"] = [abf[0:1, 36864:37376], abf[0:1, 37888:38400]]
            B["gabf"] = abf[0:1, 38400:38912]
            B["gbbf"] = abf[0:1, 38912:39424]
            B["sA1"] = [af32[:, 0:512], af32[:, 1024:1536]]
            B["sA2"] = [af32[:, 512:1024], af32[:, 1536:2048]]
            B["t1f"] = af32[:, 2048:2560]
            B["t2f"] = af32[:, 2560:3072]
            B["tAf"] = af32[:, 3072:3584]
            B["r1row"] = [af32[0:1, 3584:4096], af32[0:1, 4608:5120]]
            B["r2row"] = [af32[0:1, 4096:4608], af32[0:1, 5120:5632]]
            B["gmu"] = af32[0:1, 5632:6144]
            B["gsc"] = af32[0:1, 6144:6656]
            B["grr"] = af32[0:1, 6656:7168]
            B["onec"], B["oner"] = onec_g, oner_g
            B["epsc"] = epsc_g
            B["neglam"] = neglam_g
            B["ggm"], B["gbt"] = ggm_g, gbt_g
            return B

        BB = mk_attn_B()

        with nc.Block() as block:
            @block.tensor
            def _(t):
                g, v, a, sy = nc.gpsimd, nc.vector, nc.scalar, nc.sync
                for dst, src in ((onec_g[:, :], onec), (oner_g[0:1, :], oner),
                                 (identf_g[:, :], identf),
                                 (epsc_g[:, :], epsc), (gb_g[:, :], gbi),
                                 (wg2T_g[0:8, :], wg2Ti),
                                 (bq_g[:, :], bqc), (bk_g[:, :], bkc),
                                 (bv_g[:, :], bvc), (bo_g[:, :], boc),
                                 (bupc_g[:, :], bupci),
                                 (ggm_g[:, :], ggmi), (gbt_g[:, :], gbti),
                                 (lam_g[0][:, :], lq1), (lam_g[1][:, :], lk1),
                                 (lam_g[2][:, :], lq2), (lam_g[3][:, :], lk2)):
                    em.dma(g, dst, src, "cst")
                em.dma(g, embT_g[:, :].rearrange("p (a h) -> p a h", a=NDC),
                       embT.rearrange("(a p) h -> p a h", p=128), "cst")
                em.dma(g, wg1T_g[:, :].rearrange("p (a h) -> p a h", a=NDC),
                       wg1Ti.rearrange("(a p) h -> p a h", p=128), "cst")
                for e_ in (t, v, a):
                    em.wait(e_, "cst")
                em.dv(v.tensor_mul(lsc1_g[:, 0:1], lam_g[0][:, 0:1],
                                   lam_g[1][:, 0:1]))
                em.dv(v.tensor_mul(lsc2_g[:, 0:1], lam_g[2][:, 0:1],
                                   lam_g[3][:, 0:1]))
                em.wait(a, "dv")
                em.acs(a.activation(lsc1_g[:, 0:1], lsc1_g[:, 0:1], AF.Exp))
                em.acs(a.activation(lsc2_g[:, 0:1], lsc2_g[:, 0:1], AF.Exp))
                em.wait(v, "ac")
                em.dv(v.tensor_sub(neglam_g[:, 0:1], lsc1_g[:, 0:1],
                                   lsc2_g[:, 0:1]))
                em.dv(v.tensor_scalar_add(neglam_g[:, 0:1], neglam_g[:, 0:1],
                                          LAM_INIT))
                em.dv(v.tensor_scalar_mul(neglam_g[:, 0:1], neglam_g[:, 0:1],
                                          -1.0))
                em.dv(v.tensor_scalar_mul(ggm_g[:, 0:NH], ggm_g[:, 0:NH],
                                          1.0 - LAM_INIT))
                em.dv(v.tensor_scalar_mul(gbt_g[:, 0:NH], gbt_g[:, 0:NH],
                                          1.0 - LAM_INIT))
                engs = [t, v, a, g, sy]
                idx = [e.compute_val(e.partition_id() % 2) for e in engs]
                OWNOFF = [[0, 0, 512, 512], [512, 512, 0, 0]]
                QGSET = [[0, 2, 5, 7], [1, 3, 4, 6]]

                for tt in range(NTT_F):
                    t0 = tt * TT
                    nc.all_engine_barrier()
                    emit_ln(em, CA, x0, t0, TT, CA["xn"])
                    nc.all_engine_barrier()
                    emit_gate0(em, CA, CA["xn"], TT, cwT_g)
                    nc.all_engine_barrier()

                    def qkv0_handler(ot, pso, oslot):
                        s, head = divmod(ot, NH)
                        if s < 2:
                            dst = q0d if s == 0 else k0d
                            par = ot % 2
                            eq = CA["evq2"][par]
                            eqs = CA["evqs2"][par]
                            em.wait(v, "pe")
                            eqs.begin(v)
                            em.dv(v.tensor_copy(eq[:, 0:TT], pso[:, 0:TT]))
                            oslot.done("dv")
                            em.wait(sy, "dv")
                            em.dma(sy, dst[head * 128:(head + 1) * 128,
                                           t0:t0 + TT], eq[:, 0:TT],
                                   f"sq1{par}")
                            eqs.done(f"sq1{par}")
                        else:
                            _v_store(em, CA, pso, oslot, v0d, head, t0, None)

                    emit_proj(em, CA, wsA, wqkvT, CA["xn"], TT, 3 * D,
                              qkv0_handler)
                    nc.all_engine_barrier()
                    emit_ffn0(em, CA, CA["xn"], x0, t0, TT, cwT_g,
                              [wgT[e] for e in range(E0)],
                              [wdT[e] for e in range(E0)],
                              [wuT[e] for e in range(E0)],
                              swgT, swdT, swuT, CA["acc"], acc0d)
                nc.all_engine_barrier()
                for s_ in ("sq10", "sq11", "sq20", "sq21", "sacc"):
                    em.sem(s_)
                    em.wait(g, s_)
                em.dma(g, BB["msk"].rearrange("p (a x) -> p a x", a=4),
                       masksi.rearrange("a p x -> p a x"), "cst")
                em.wait(g, "cst")
                emit_attn0_phase(em, BB, q0d, k0d, v0d, attn0d)
                nc.all_engine_barrier()
                for tt in range(NTT_F):
                    t0 = tt * TT
                    nc.all_engine_barrier()
                    em.wait(g, "sat")
                    tok = em.dma(g, CC["attn_sb"].rearrange(
                        "p (a x) -> p a x", a=NDC),
                        attn0d[:, t0:t0 + TT].rearrange("(a p) x -> p a x",
                                                        p=128), "atn")
                    em.waitc(t, tok)

                    def oproj0_handler(ot, pso, oslot):
                        xb, xbs = CC["xp"][ot % 2], CC["xpslot"][ot % 2]
                        xbs.begin(g)
                        tk = em.dma(g, xb[:, 0:TT],
                                    acc0d[ot * 128:(ot + 1) * 128,
                                          t0:t0 + TT], f"xp{ot % 2}")
                        em.waitc(v, tk)
                        em.wait(v, "pe")
                        CC["x1cs"].begin(v)
                        em.dv(v.tensor_add(x1c_g[:, 0:TT], pso[:, 0:TT],
                                           xb[:, 0:TT]))
                        xbs.done("dv")
                        oslot.done("dv")
                        em.wait(g, "dv")
                        em.dma(g, x1d[ot * 128:(ot + 1) * 128, t0:t0 + TT],
                               x1c_g[:, 0:TT], "sx1")
                        CC["x1cs"].done("sx1")

                    for s_ in ("sq10", "sq11"):
                        em.sem(s_)
                        em.wait(v, s_)
                    emit_proj(em, CC, wsB, woT, CC["attn_sb"], TT, D,
                              oproj0_handler)
                    nc.all_engine_barrier()
                    em.wait(g, "sx1")
                    emit_ln(em, CC, x1d, t0, TT, CC["xn"])
                    nc.all_engine_barrier()
                    emit_gate1(em, CC, CC["xn"], TT, cwT_g)
                    nc.all_engine_barrier()

                    def mk_qk1_handler(dst, bias_sb):
                        def h_(ot, pso, oslot):
                            par = ot % 2
                            eq = CC["evq2"][par]
                            eqs = CC["evqs2"][par]
                            em.wait(v, "pe")
                            eqs.begin(v)
                            em.dv(v.tensor_single_scalar(
                                eq[:, 0:TT], pso[:, 0:TT],
                                bias_sb[:, ot:ot + 1], ALU.add))
                            oslot.done("dv")
                            em.wait(sy, "dv")
                            em.dma(sy, dst[ot * 128:(ot + 1) * 128,
                                           t0:t0 + TT], eq[:, 0:TT],
                                   f"sq1{par}")
                            eqs.done(f"sq1{par}")
                        return h_

                    def v1_handler(ot, pso, oslot):
                        _v_store(em, CC, pso, oslot, v1d, ot, t0,
                                 bv_g[:, ot:ot + 1])

                    emit_proj(em, CC, wsA, wq1, CC["xn"], TT, D,
                              mk_qk1_handler(qf1d, bq_g))
                    emit_proj(em, CC, wsA, wk1, CC["xn"], TT, D,
                              mk_qk1_handler(k1d, bk_g))
                    emit_proj(em, CC, wsA, wv1, CC["xn"], TT, D, v1_handler)
                    nc.all_engine_barrier()
                    em.wait(g, "atn")
                    bdtok = em.dma(g, CC["bdown_sb"], bdownr, "cst")
                    em.waitc(t, bdtok)
                    w_ = tt

                    def ffn1_body(h):
                        emit_ffn1h(em, CC, CC["xn"], x1d, w_ * TT,
                                   OWNOFF[h][w_], cwT_g,
                                   [wupT[e] for e in range(E1)], bupc_g,
                                   [wdownT[e] for e in range(E1)],
                                   CC["bdown_sb"], CC["acc"], acc1d)

                    switch2(em, engs, idx, ffn1_body)
                nc.all_engine_barrier()
                for s_ in ("sq10", "sq11", "sq20", "sq21", "sacc"):
                    em.wait(g, s_)
                em.dma(g, BB["msk"].rearrange("p (a x) -> p a x", a=4),
                       masksi.rearrange("a p x -> p a x"), "cst")
                em.wait(g, "cst")
                switch2(em, engs, idx,
                        lambda h: emit_attn1_phase(em, BB, qf1d, k1d, v1d,
                                                   attn1d, qgs=QGSET[h]))
                nc.all_engine_barrier()
                def e_body(h):
                    asb_slot = CC["asbslot"]
                    for tt in range(NTT_F):
                        t0 = tt * TT + OWNOFF[h][tt]
                        em.wait(g, "sat")
                        asb_slot.begin(g)
                        tok = em.dma(g, CC["attn_sb"][:, 0:NDC * 512].rearrange(
                            "p (a x) -> p a x", a=NDC),
                            attn1d[:, t0:t0 + 512].rearrange(
                                "(a p) x -> p a x", p=128), "atn")
                        em.waitc(t, tok)

                        def out_handler(ot, pso, oslot):
                            xb, xbs = CC["xp"][ot % 2], CC["xpslot"][ot % 2]
                            xbs.begin(g)
                            tk = em.dma(g, xb[:, 0:512],
                                        acc1d[ot * 128:(ot + 1) * 128,
                                              t0:t0 + 512], f"xp{ot % 2}")
                            em.waitc(v, tk)
                            em.wait(v, "pe")
                            CC["ot1s"].begin(v)
                            em.dv(v.tensor_single_scalar(
                                CC["evv"][:, 0:512], pso[:, 0:512],
                                bo_g[:, ot:ot + 1], ALU.add))
                            oslot.done("dv")
                            em.dv(v.tensor_add(CC["evv"][:, 0:512],
                                               CC["evv"][:, 0:512],
                                               xb[:, 0:512]))
                            xbs.done("dv")
                            em.wait(sy, "dv")
                            em.dma(sy, xo[ot * 128:(ot + 1) * 128,
                                          t0:t0 + 512],
                                   CC["evv"][:, 0:512], "sxo", store=True)
                            CC["ot1s"].done("sxo")

                        emit_proj(em, CC, wsA, wo1, CC["attn_sb"], 512, D,
                                  out_handler)
                        asb_slot.done("pe")

                switch2(em, engs, idx, e_body)
                nc.all_engine_barrier()
                em.final_store_waits([nc.gpsimd, nc.sync])
    return nc


def _v_store(em, C, pso, oslot, vdst, head, t0, bias_col):
    nc = em.nc
    t, v, sy = nc.tensor, nc.vector, nc.sync
    par = head % 2
    evv = C["evv2"][par]
    vtm = C["vtm2"][par]
    evvs = C["evvs2"][par]
    vtms = C["vtms2"][par]
    em.wait(v, "pe")
    evvs.begin(v)
    if bias_col is None:
        em.dv(v.tensor_copy(evv[:, 0:TT], pso[:, 0:TT]))
    else:
        em.dv(v.tensor_single_scalar(evv[:, 0:TT], pso[:, 0:TT],
                                     bias_col, ALU.add))
    oslot.done("dv")
    em.wait(t, "dv")
    C["pDslot"].begin(t)
    for i in range(TT // 128):
        ins = t.transpose(C["pD"][:, i * 128:(i + 1) * 128],
                          evv[:, i * 128:(i + 1) * 128],
                          C["identf"][:, :])
    em.inc("pe", ins)
    evvs.done("pe")
    em.wait(v, "pe")
    vtms.begin(v)
    em.dv(v.tensor_copy(vtm[:, 0:TT], C["pD"][:, 0:TT]))
    C["pDslot"].done("dv")
    em.wait(sy, "dv")
    em.dma(sy, vdst[t0:t0 + TT, head * 128:(head + 1) * 128].rearrange(
        "(i p) d -> p i d", p=128),
        vtm[:, 0:TT].rearrange("p (i d) -> p i d", d=128), f"sq2{par}")
    vtms.done(f"sq2{par}")


def prep_fused_inputs(inputs, core):
    b = core // 2
    x = np.asarray(inputs["L0_attn_x"])
    c = host_consts()
    bup = np.asarray(inputs["L1_ffn_bup"], np.float32)
    bupc = np.concatenate([bup[e].reshape(8, 128).T for e in range(8)], axis=1)
    return {
        "x0": np.ascontiguousarray(x[b].T.astype(np.float32)),
        "wqkvT": _wt(inputs["L0_attn_W_qkv"]),
        "embT": _wt(inputs["L0_ffn_expert_emb"]),
        "gb": np.ascontiguousarray(
            np.broadcast_to(np.asarray(inputs["L0_ffn_gate_bias"],
                                       np.float32)[None, :], (128, 8))),
        "wgT": _wt3(inputs["L0_ffn_Wg"]),
        "wdT": _wt3(inputs["L0_ffn_Wd"]),
        "wuT": _wt3(inputs["L0_ffn_Wu"]),
        "swgT": _wt(inputs["L0_ffn_sWg"]),
        "swdT": _wt(inputs["L0_ffn_sWd"]),
        "swuT": _wt(inputs["L0_ffn_sWu"]),
        "lq1": np.asarray(inputs["L0_attn_lambda_q1"],
                          np.float32).reshape(128, 1),
        "lk1": np.asarray(inputs["L0_attn_lambda_k1"],
                          np.float32).reshape(128, 1),
        "lq2": np.asarray(inputs["L0_attn_lambda_q2"],
                          np.float32).reshape(128, 1),
        "lk2": np.asarray(inputs["L0_attn_lambda_k2"],
                          np.float32).reshape(128, 1),
        "ggm": np.ascontiguousarray(
            np.asarray(inputs["L0_attn_gn_gamma"], np.float32).T),
        "gbt": np.ascontiguousarray(
            np.asarray(inputs["L0_attn_gn_beta"], np.float32).T),
        "masks": c["masks2"],
        "woT": _wt(inputs["L0_attn_W_o"]),
        "wq1": np.ascontiguousarray(np.asarray(inputs["L1_attn_Wq"],
                                               np.float32)),
        "wk1": np.ascontiguousarray(np.asarray(inputs["L1_attn_Wk"],
                                               np.float32)),
        "wv1": np.ascontiguousarray(np.asarray(inputs["L1_attn_Wv"],
                                               np.float32)),
        "bqc": _bcol(inputs["L1_attn_bq"], 16),
        "bkc": _bcol(inputs["L1_attn_bk"], 16),
        "bvc": _bcol(inputs["L1_attn_bv"], 16),
        "wg1T": _wt(inputs["L1_ffn_Wg1"]),
        "wg2T": _wt(inputs["L1_ffn_Wg2"]),
        "wupT": _wt3(inputs["L1_ffn_Wup"]),
        "bupc": np.ascontiguousarray(bupc),
        "wdownT": _wt3(inputs["L1_ffn_Wdown"]),
        "bdownr": np.ascontiguousarray(np.asarray(inputs["L1_ffn_bdown"],
                                                  np.float32)),
        "wo1": np.ascontiguousarray(np.asarray(inputs["L1_attn_Wo"],
                                               np.float32)),
        "boc": _bcol(inputs["L1_attn_bo"], 16),
        "onec": c["onec"], "oner": c["oner"], "identf": c["identf"],
        "epsc": np.full((128, 1), LN_EPS, np.float32),
    }


QGSET_HOST = [[0, 2, 5, 7], [1, 3, 4, 6]]


def kernel(**inputs):
    cores = list(range(8))
    nc = build_fused()
    res = run_bass_kernel_spmd(
        nc, [prep_fused_inputs(inputs, c) for c in cores], core_ids=cores)
    out = np.empty((4, 4096, 2048), np.float32)
    for c in cores:
        b, h = divmod(c, 2)
        xoc = np.asarray(res.results[c]["xo"]).T
        for qg in QGSET_HOST[h]:
            out[b, qg * 512:(qg + 1) * 512, :] = \
                xoc[qg * 512:(qg + 1) * 512, :]
    return out
